# version 02
# baseline (speedup 1.0000x reference)
import math
import numpy as np
import ml_dtypes
import concourse.bass as bass
import concourse.mybir as mybir
from concourse.bass_utils import run_bass_kernel_spmd

F32 = mybir.dt.float32
BF16 = mybir.dt.bfloat16
AF = mybir.ActivationFunctionType
ALU = mybir.AluOpType
AX = mybir.AxisListType

B, T, Cd = 2, 4096, 2048
H, Dh = 16, 128
BLK = 64
NQB = 16
QTOK = NQB * BLK
SCALE = 1.0 / math.sqrt(Dh)
EPS = 1e-5

N_WIN_T, N_GLOB_T, N_DIL_T = 16, 1, 4
NT_W = N_WIN_T + N_GLOB_T + N_DIL_T
NKT = NT_W * 128
DIL_BLOCKS = [4, 8, 12, 16, 20, 24, 28, 28]

SLOTS = [list(range(0, 12)) + [16] + [17, 18, 19, 20],
         list(range(4, 16)) + [16] + [17, 18, 19, 20]]
NSLOT = 17

A1_CHUNKS = [(0, 512), (512, 512), (1024, 512), (1536, 512), (2048, 512), (2560, 128)]

MOE_E, I_MOE, I_SH = 8, 1408, 5632
GU_FC, DN_FC, DN_KC = 22, 16, 11
SH_FC, SHD_KC = 44, 44
KC = 16


def ws_blocks(r):
    win = [16 * r - 16 + i for i in range(32)]
    return win + [0, 1] + list(DIL_BLOCKS)


def block_allowed(qb, kb):
    if kb > qb:
        return False
    return (qb - kb <= 16) or (kb <= 1) or (kb % 4 == 0)


def make_masks():
    all_masks = []
    for r in range(4):
        wb = ws_blocks(r)
        m = np.zeros((2, NSLOT, 128, 8), dtype=np.float32)
        for ch in range(2):
            provider = {}
            for si, s in enumerate(SLOTS[ch]):
                for half in range(2):
                    kb = wb[2 * s + half]
                    if kb >= 0 and kb not in provider:
                        provider[kb] = (si, half)
            for si, s in enumerate(SLOTS[ch]):
                for half in range(2):
                    kb = wb[2 * s + half]
                    if kb < 0 or provider.get(kb) != (si, half):
                        continue
                    for q8 in range(8):
                        qb = 16 * r + 8 * ch + q8
                        if block_allowed(qb, kb):
                            m[ch, si, half * 64:(half + 1) * 64, q8] = 1.0
        all_masks.append(m)
    need = [[False] * NSLOT for _ in range(2)]
    for ch in range(2):
        for si in range(NSLOT):
            if any((all_masks[r][ch, si] != 1.0).any() for r in range(4)):
                need[ch][si] = True
    return all_masks, need


def rope_tables(positions):
    pos = np.asarray(positions, dtype=np.float64)
    inv = 1.0 / (10000.0 ** (np.arange(0, Dh, 2, dtype=np.float64) / Dh))
    inv2 = np.concatenate([inv, inv])
    ang = np.outer(inv2, np.clip(pos, 0, None))
    cos = np.cos(ang).astype(np.float32)
    sin = np.sin(ang).astype(np.float32)
    z = pos < 0
    cos[:, z] = 0.0
    sin[:, z] = 0.0
    return cos, sin


def retile_lhsT(wT, n_kc, n_fc):
    K, F = wT.shape
    assert K == n_kc * 128 and F == n_fc * 128
    return np.ascontiguousarray(
        wT.reshape(n_kc, 128, n_fc, 128).transpose(2, 1, 0, 3))


def stage_host(inputs):
    x = np.asarray(inputs["L0_attn_x"], dtype=np.float32)
    wq = np.asarray(inputs["L0_attn_w_q"], dtype=np.float32)
    wk = np.asarray(inputs["L0_attn_w_k"], dtype=np.float32)
    wv = np.asarray(inputs["L0_attn_w_v"], dtype=np.float32)
    wo = np.asarray(inputs["L0_attn_w_o"], dtype=np.float32)
    rww = np.asarray(inputs["L0_ffn_router_weight"], dtype=np.float32)
    gu = np.asarray(inputs["L0_ffn_gate_up_proj"], dtype=np.float32)
    dwn = np.asarray(inputs["L0_ffn_down_proj"], dtype=np.float32)
    shg = np.asarray(inputs["L0_ffn_sh_gate_w"], dtype=np.float32)
    shu = np.asarray(inputs["L0_ffn_sh_up_w"], dtype=np.float32)
    shd = np.asarray(inputs["L0_ffn_sh_down_w"], dtype=np.float32)
    sgw = np.asarray(inputs["L0_ffn_shared_gate_w"], dtype=np.float32)

    shared = {
        "wq_t": retile_lhsT(wq.T, KC, 16),
        "wk_t": retile_lhsT(wk.T, KC, 16),
        "wv_t": np.ascontiguousarray(wv.T.reshape(KC, 128, 2048).transpose(1, 0, 2)),
        "wo_t": retile_lhsT(wo.T, KC, 16),
        "rw_t": np.ascontiguousarray(rww.T.reshape(KC, 128, 8).transpose(1, 0, 2)),
        "sgw_t": np.ascontiguousarray(sgw.T.reshape(KC, 128, 1).transpose(1, 0, 2)),
        "gu_t": np.stack([retile_lhsT(gu[e].T, KC, GU_FC) for e in range(MOE_E)]),
        "dn_t": np.stack([retile_lhsT(dwn[e].T, DN_KC, DN_FC) for e in range(MOE_E)]),
        "shg_t": retile_lhsT(shg.T, KC, SH_FC),
        "shu_t": retile_lhsT(shu.T, KC, SH_FC),
        "shd_t": retile_lhsT(shd.T, SHD_KC, DN_FC),
    }
    masks, mask_need = make_masks()

    in_maps = []
    for c in range(8):
        b, r = c // 4, c % 4
        xT = x[b].T
        wb = ws_blocks(r)
        xkv = np.zeros((Cd, NKT), dtype=np.float32)
        positions = np.full(NKT, -1, dtype=np.int64)
        for i, blk in enumerate(wb):
            if blk >= 0:
                xkv[:, BLK * i:BLK * (i + 1)] = xT[:, BLK * blk:BLK * (blk + 1)]
                positions[BLK * i:BLK * (i + 1)] = np.arange(BLK * blk, BLK * (blk + 1))
        cosv, sinv = rope_tables(positions)
        m = {
            "xkv": np.ascontiguousarray(xkv.reshape(KC, 128, NKT).transpose(1, 0, 2)),
            "ck": cosv, "sk": sinv,
            "msk": np.ascontiguousarray(masks[r].transpose(2, 0, 1, 3).reshape(128, 2 * NSLOT * 8)),
        }
        m.update(shared)
        in_maps.append(m)
    return in_maps, mask_need


class Buf:
    __slots__ = ("name", "wtok", "rtoks")

    def __init__(self, name):
        self.name = name
        self.wtok = None
        self.rtoks = []


class TV:
    __slots__ = ("buf", "ap")

    def __init__(self, buf, ap):
        self.buf = buf
        self.ap = ap

    def __getitem__(self, idx):
        return TV(self.buf, self.ap[idx])

    def view(self, ap):
        return TV(self.buf, ap)


ENG2BLOCK = {"pe": "tensor", "act": "scalar", "dve": "vector", "pool": "gpsimd", "sp": "sync"}
QUEUE_OF = {"sp": "qsp", "act": "qact", "pool": "qpool"}


class Sched:
    def __init__(self, nc):
        self.nc = nc
        self.ops = {e: [] for e in ENG2BLOCK}
        self.cnt = {}
        self.obs = {e: {} for e in ENG2BLOCK}
        self.meta = {}
        self.sems = {}
        for s in ["pe", "act", "dve", "pool", "sp", "qsp", "qact", "qpool"]:
            self.sems[s] = nc.alloc_semaphore(f"sem_{s}")

    def _needs(self, reads, writes):
        needs = {}

        def add(tok):
            if tok is not None:
                s, v = tok
                if needs.get(s, 0) < v:
                    needs[s] = v
        for b in reads:
            if b is not None:
                add(b.wtok)
        for b in writes:
            if b is not None:
                add(b.wtok)
                for t in b.rtoks:
                    add(t)
        return needs

    def emit(self, eng, fn, reads=(), writes=(), dma=False):
        needs = self._needs(reads, writes)
        obs = self.obs[eng]
        waits = []
        for sem in sorted(needs):
            val = needs[sem]
            if eng == "pe" and sem == "pe":
                continue
            if obs.get(sem, 0) >= val:
                continue
            waits.append((sem, val))
            obs[sem] = val
            mclock = self.meta.get((sem, val))
            if mclock:
                for s2, v2 in mclock.items():
                    if obs.get(s2, 0) < v2:
                        obs[s2] = v2
        if dma:
            q = QUEUE_OF[eng]
            self.cnt[q] = self.cnt.get(q, 0) + 16
            tok = (q, self.cnt[q])
        else:
            self.cnt[eng] = self.cnt.get(eng, 0) + 1
            tok = (eng, self.cnt[eng])
        self.meta[tok] = dict(obs)
        self.ops[eng].append((waits, fn, tok))
        for b in writes:
            if b is not None:
                b.wtok = tok
                b.rtoks = []
        for b in reads:
            if b is not None:
                b.rtoks.append(tok)
        return tok

    def dma(self, eng, out_tv, in_tv, extra_reads=()):
        out_b = out_tv.buf if isinstance(out_tv, TV) else None
        in_b = in_tv.buf if isinstance(in_tv, TV) else None
        out_ap = out_tv.ap if isinstance(out_tv, TV) else out_tv
        in_ap = in_tv.ap if isinstance(in_tv, TV) else in_tv
        return self.emit(eng, lambda e, o=out_ap, i=in_ap: e.dma_start(out=o, in_=i),
                         reads=[in_b] + list(extra_reads), writes=[out_b], dma=True)

    def mm(self, out_tv, lhsT_tv, rhs_tv, start, stop):
        return self.emit("pe", lambda e, o=out_tv.ap, l=lhsT_tv.ap, r=rhs_tv.ap,
                         st=start, sp=stop: e.matmul(o, l, r, start=st, stop=sp),
                         reads=[lhsT_tv.buf, rhs_tv.buf], writes=[out_tv.buf])

    def act(self, out_tv, in_tv, func, bias=0.0, scale=1.0, accum=None):
        reads = [in_tv.buf]
        bias_ap = bias
        if isinstance(bias, TV):
            reads.append(bias.buf)
            bias_ap = bias.ap
        writes = [out_tv.buf]
        acc_ap = None
        if accum is not None:
            writes.append(accum.buf)
            acc_ap = accum.ap
        return self.emit("act", lambda e, o=out_tv.ap, i=in_tv.ap, f=func, b=bias_ap,
                         s=scale, a=acc_ap: e.activation(o, i, f, bias=b, scale=s, accum_out=a)
                         if a is not None else e.activation(o, i, f, bias=b, scale=s),
                         reads=reads, writes=writes)

    def tt(self, eng, out_tv, a_tv, b_tv, op):
        return self.emit(eng, lambda e, o=out_tv.ap, x=a_tv.ap, y=b_tv.ap, p=op:
                         e.tensor_tensor(o, x, y, p),
                         reads=[a_tv.buf, b_tv.buf], writes=[out_tv.buf])

    def ts(self, eng, out_tv, in_tv, s1, op, s2=None, op2=None):
        reads = [in_tv.buf]
        s1ap = s1
        if isinstance(s1, TV):
            reads.append(s1.buf)
            s1ap = s1.ap
        s2ap = s2
        if isinstance(s2, TV):
            reads.append(s2.buf)
            s2ap = s2.ap
        if op2 is None:
            return self.emit(eng, lambda e, o=out_tv.ap, i=in_tv.ap, a=s1ap, p=op:
                             e.tensor_scalar(o, i, a, None, p),
                             reads=reads, writes=[out_tv.buf])
        return self.emit(eng, lambda e, o=out_tv.ap, i=in_tv.ap, a=s1ap, b=s2ap, p=op, p2=op2:
                         e.tensor_scalar(o, i, a, b, p, p2),
                         reads=reads, writes=[out_tv.buf])

    def stt(self, eng, out_tv, in0_tv, scalar, in1_tv, op0, op1):
        reads = [in0_tv.buf, in1_tv.buf]
        sap = scalar
        if isinstance(scalar, TV):
            reads.append(scalar.buf)
            sap = scalar.ap
        return self.emit(eng, lambda e, o=out_tv.ap, i0=in0_tv.ap, s=sap, i1=in1_tv.ap,
                         p0=op0, p1=op1: e.scalar_tensor_tensor(o, i0, s, i1, p0, p1),
                         reads=reads, writes=[out_tv.buf])

    def copy(self, eng, out_tv, in_tv):
        if eng == "act":
            return self.emit(eng, lambda e, o=out_tv.ap, i=in_tv.ap: e.copy(o, i),
                             reads=[in_tv.buf], writes=[out_tv.buf])
        return self.emit(eng, lambda e, o=out_tv.ap, i=in_tv.ap: e.tensor_copy(o, i),
                         reads=[in_tv.buf], writes=[out_tv.buf])

    def red(self, out_tv, in_tv, op, negate=False):
        if op == "max":
            return self.emit("dve", lambda e, o=out_tv.ap, i=in_tv.ap, n=negate:
                             e.reduce_max(o, i, axis=AX.X, negate=n),
                             reads=[in_tv.buf], writes=[out_tv.buf])
        return self.emit("dve", lambda e, o=out_tv.ap, i=in_tv.ap:
                         e.reduce_sum(o, i, axis=AX.X),
                         reads=[in_tv.buf], writes=[out_tv.buf])

    def recip(self, out_tv, in_tv):
        return self.emit("dve", lambda e, o=out_tv.ap, i=in_tv.ap: e.reciprocal(o, i),
                         reads=[in_tv.buf], writes=[out_tv.buf])

    def memset(self, eng, out_tv, val):
        return self.emit(eng, lambda e, o=out_tv.ap, v=val: e.memset(o, v),
                         reads=[], writes=[out_tv.buf])

    def alias(self, new_bufs, old_bufs):
        toks = []
        for b in old_bufs:
            if b is None:
                continue
            if b.wtok:
                toks.append(b.wtok)
            toks.extend(b.rtoks)
        for nb in new_bufs:
            nb.rtoks = list(toks)
            nb.wtok = None

    def replay(self):
        nc = self.nc
        sems = self.sems
        with nc.Block() as block:
            for eng, section in ENG2BLOCK.items():
                ops = self.ops[eng]
                final = dict(self.cnt) if eng == "sp" else None

                def body(e, ops=ops, eng=eng, final=final):
                    for waits, fn, tok in ops:
                        for (s, v) in waits:
                            e.wait_ge(sems[s], v)
                        inst = fn(e)
                        inst.then_inc(sems[tok[0]], 16 if tok[0].startswith("q") else 1)
                    if final is not None:
                        for s, v in sorted(final.items()):
                            e.wait_ge(sems[s], v)
                getattr(block, section)(body)


def build_program(mask_need, taps=()):
    nc = bass.Bass("TRN2", target_bir_lowering=False)

    d_xkv = nc.dram_tensor("xkv", [128, KC, NKT], F32, kind="ExternalInput").ap()
    d_ck = nc.dram_tensor("ck", [128, NKT], F32, kind="ExternalInput").ap()
    d_sk = nc.dram_tensor("sk", [128, NKT], F32, kind="ExternalInput").ap()
    d_msk = nc.dram_tensor("msk", [128, 2 * NSLOT * 8], F32, kind="ExternalInput").ap()
    d_wq = nc.dram_tensor("wq_t", [16, 128, KC, 128], F32, kind="ExternalInput").ap()
    d_wk = nc.dram_tensor("wk_t", [16, 128, KC, 128], F32, kind="ExternalInput").ap()
    d_wv = nc.dram_tensor("wv_t", [128, KC, 2048], F32, kind="ExternalInput").ap()
    d_wo = nc.dram_tensor("wo_t", [16, 128, KC, 128], F32, kind="ExternalInput").ap()
    d_rw = nc.dram_tensor("rw_t", [128, KC, 8], F32, kind="ExternalInput").ap()
    d_sgw = nc.dram_tensor("sgw_t", [128, KC, 1], F32, kind="ExternalInput").ap()
    d_gu = nc.dram_tensor("gu_t", [MOE_E, GU_FC, 128, KC, 128], F32, kind="ExternalInput").ap()
    d_dn = nc.dram_tensor("dn_t", [MOE_E, DN_FC, 128, DN_KC, 128], F32, kind="ExternalInput").ap()
    d_shg = nc.dram_tensor("shg_t", [SH_FC, 128, KC, 128], F32, kind="ExternalInput").ap()
    d_shu = nc.dram_tensor("shu_t", [SH_FC, 128, KC, 128], F32, kind="ExternalInput").ap()
    d_shd = nc.dram_tensor("shd_t", [DN_FC, 128, SHD_KC, 128], F32, kind="ExternalInput").ap()
    d_out = nc.dram_tensor("out", [16, 128, QTOK], F32, kind="ExternalOutput").ap()

    kfm_kind = "ExternalOutput" if "kfm" in taps else "Internal"
    vtm_kind = "ExternalOutput" if "vtm" in taps else "Internal"
    d_kfm = nc.dram_tensor("kfm_d", [16, 128, NKT], BF16, kind=kfm_kind).ap()
    d_vtm = nc.dram_tensor("vtm_d", [NKT, 2048], BF16, kind=vtm_kind).ap()
    tap_aps = {}
    for tname, shape, dt_ in [("ctx", [128, 16 * QTOK], BF16),
                              ("r1n", [128, 16 * QTOK], BF16),
                              ("logits", [128, 64], F32),
                              ("combine", [128, 64], F32),
                              ("acc", [128, 16 * QTOK], F32),
                              ("hsh", [128, SH_FC * 512], BF16),
                              ("q0", [128, QTOK], BF16),
                              ("r1", [128, 16 * 512], F32)]:
        if tname in taps:
            tap_aps[tname] = nc.dram_tensor("tap_" + tname, shape, dt_, kind="ExternalOutput").ap()

    d_ones_row = nc.inline_tensor(np.ones((1, 128), dtype=np.float32), name="ones_row").ap()
    d_ident = nc.inline_tensor(np.eye(128, dtype=np.float32), name="ident_f").ap()

    sb_C = nc.alloc_sbuf_tensor("sb_C", [128, 1024], F32)
    sb_S = nc.alloc_sbuf_tensor("sb_S", [128, 7680], F32)
    sb_P = nc.alloc_sbuf_tensor("sb_P", [128, 13824], F32)
    sb_Q = nc.alloc_sbuf_tensor("sb_Q", [128, 13824], F32)
    sb_R = nc.alloc_sbuf_tensor("sb_R", [128, 16384], F32)
    C_f, C_b = sb_C.ap(), sb_C.ap().bitcast(BF16)
    S_f, S_b = sb_S.ap(), sb_S.ap().bitcast(BF16)
    P_f, P_b = sb_P.ap(), sb_P.ap().bitcast(BF16)
    Q_f, Q_b = sb_Q.ap(), sb_Q.ap().bitcast(BF16)
    R_f, R_b = sb_R.ap(), sb_R.ap().bitcast(BF16)

    psA = nc.alloc_psum_tensor("psA", [128, 1024], F32)
    psB = nc.alloc_psum_tensor("psB", [128, 1024], F32)
    psC = nc.alloc_psum_tensor("psC", [128, 512], F32)
    psD = nc.alloc_psum_tensor("psD", [128, 512], F32)
    psE = nc.alloc_psum_tensor("psE", [128, 512], F32)
    psF = nc.alloc_psum_tensor("psF", [128, 512], F32)

    S = Sched(nc)

    def tv(name, ap):
        return TV(Buf(name), ap)

    bA0 = tv("psA0", psA.ap()[:, 0:512]); bA1 = tv("psA1", psA.ap()[:, 512:1024])
    bB0 = tv("psB0", psB.ap()[:, 0:512]); bB1 = tv("psB1", psB.ap()[:, 512:1024])
    bC = tv("psC", psC.ap()); bD = tv("psD", psD.ap())
    bE = tv("psE", psE.ap()); bF = tv("psF", psF.ap())

    ones_bf = tv("ones_bf", C_b[:, 0:1])
    sgw_sb = tv("sgw_sb", C_b[:, 8:24])
    ones_row = tv("ones_row", C_f[0:1, 16:144])
    ident = tv("ident", C_f[:, 144:272])
    masks_sb = tv("masks", C_b[:, 544:816])
    rows = tv("rows", C_f[0:1, 544:560])
    k2m = tv("k2m", C_f[0:1, 560:576])
    bias_sb = [tv("bias0", C_f[:, 576:577]), tv("bias1", C_f[:, 577:578])]
    m1c = tv("m1c", C_f[:, 578:579]); m1n = tv("m1n", C_f[:, 579:580])
    m2v = tv("m2v", C_f[:, 580:581]); denr = tv("denr", C_f[:, 581:582])
    rdenr = tv("rdenr", C_f[:, 582:583])
    comb = tv("comb", C_f[:, 586:650])
    lgs = tv("lgs", C_f[:, 650:658])
    t2a = tv("t2a", C_f[:, 658:666])
    t2b = tv("t2b", C_f[:, 666:674])
    ptl = tv("ptl", C_f[:, 674:682])
    rw_sb = tv("rw_sb", C_f[:, 682:810])

    S.memset("pool", ones_bf, 1.0)
    S.dma("sp", ones_row, d_ones_row)
    S.dma("sp", ident, d_ident)
    S.dma("pool", masks_sb.view(masks_sb.ap), d_msk)
    S.dma("sp", rw_sb.view(rw_sb.ap.rearrange("p (k e) -> p k e", e=8)), d_rw)
    S.dma("pool", sgw_sb.view(sgw_sb.ap.rearrange("p (k o) -> p k o", o=1)), d_sgw)

    kfm_bufs = [Buf(f"kfm{h}") for h in range(16)]
    vtm_bufs = [Buf(f"vtm{g}") for g in range(4)]

    wk_half = tv("wk_half", P_b[:, 0:8 * KC * 128])
    wv_half = tv("wv_half", Q_b[:, 0:KC * 1024])
    x_db = [tv(f"x{i}", R_b[:, i * KC * 512:(i + 1) * KC * 512]) for i in range(2)]
    cs_db = [(tv(f"cos{i}", R_f[:, 8192 + i * 1024:8192 + i * 1024 + 512]),
              tv(f"sin{i}", R_f[:, 8192 + i * 1024 + 512:8192 + i * 1024 + 1024]))
             for i in range(2)]
    ropeA = tv("ropeA", R_f[:, 10240:10752])
    ropeB = tv("ropeB", R_f[:, 10752:11264])
    kst_db = [tv(f"kst{i}", R_b[:, 22528 + i * 512:22528 + (i + 1) * 512]) for i in range(2)]
    vst_db = [tv(f"vst{i}", R_b[:, 23552 + i * 512:23552 + (i + 1) * 512]) for i in range(2)]
    sqst = tv("sqst", R_b[:, 24576:25088])
    k2tmp = tv("k2tmp", C_f[0:1, 583:584])

    for pas in range(2):
        wk4 = wk_half.ap.rearrange("p (fc kc f) -> p fc kc f", fc=8, kc=KC)
        wk_src = d_wk[8 * pas:8 * pas + 8].rearrange("fc p kc f -> p fc kc f")
        wk_srcf = d_wk[8 * pas:8 * pas + 8].rearrange("fc p kc f -> p fc (kc f)")
        S.dma("pool", wk_half.view(wk_half.ap[:, 0:8192].rearrange("p (a b) -> p a b", a=4)), wk_srcf[:, 0:4])
        S.dma("pool", wk_half.view(wk_half.ap[:, 8192:16384].rearrange("p (a b) -> p a b", a=4)), wk_srcf[:, 4:8])
        wv3 = wv_half.ap.rearrange("p (kc d) -> p kc d", kc=KC)
        wv_src = d_wv[:, :, 1024 * pas:1024 * pas + 1024]
        S.dma("pool", wv_half.view(wv3[:, 0:8]), wv_src[:, 0:8])
        S.dma("pool", wv_half.view(wv3[:, 8:16]), wv_src[:, 8:16])
        kps = [bA0, bA1]
        vps = [bB0, bB1]
        for ci, (c0, cw) in enumerate(A1_CHUNKS):
            xc = x_db[ci % 2]
            xc3 = xc.ap.rearrange("p (kc t) -> p kc t", kc=KC)
            S.dma("pool", xc.view(xc3[:, :, 0:cw]), d_xkv[:, :, c0:c0 + cw])
            cosc, sinc = cs_db[ci % 2]
            S.dma("sp", cosc.view(cosc.ap[:, 0:cw]), d_ck[:, c0:c0 + cw])
            S.dma("sp", sinc.view(sinc.ap[:, 0:cw]), d_sk[:, c0:c0 + cw])
            lo, hi = slice(0, 64), slice(64, 128)
            for fi in range(8):
                h = 8 * pas + fi
                pk = kps[fi % 2]
                pko = pk.view(pk.ap[:, 0:cw])
                for kc in range(KC):
                    S.mm(pko, wk_half.view(wk_half.ap[:, (fi * KC + kc) * 128:(fi * KC + kc + 1) * 128]),
                         xc.view(xc3[:, kc, 0:cw]), start=(kc == 0), stop=(kc == KC - 1))
                kst = kst_db[fi % 2]
                S.tt("dve", ropeA.view(ropeA.ap[lo, 0:cw]), pk.view(pk.ap[lo, 0:cw]),
                     cosc.view(cosc.ap[lo, 0:cw]), ALU.mult)
                S.tt("dve", ropeB.view(ropeB.ap[lo, 0:cw]), pk.view(pk.ap[hi, 0:cw]),
                     sinc.view(sinc.ap[lo, 0:cw]), ALU.mult)
                S.tt("dve", kst.view(kst.ap[lo, 0:cw]), ropeA.view(ropeA.ap[lo, 0:cw]),
                     ropeB.view(ropeB.ap[lo, 0:cw]), ALU.subtract)
                S.tt("dve", ropeA.view(ropeA.ap[hi, 0:cw]), pk.view(pk.ap[hi, 0:cw]),
                     cosc.view(cosc.ap[hi, 0:cw]), ALU.mult)
                S.tt("dve", ropeB.view(ropeB.ap[hi, 0:cw]), pk.view(pk.ap[lo, 0:cw]),
                     sinc.view(sinc.ap[hi, 0:cw]), ALU.mult)
                S.tt("dve", kst.view(kst.ap[hi, 0:cw]), ropeA.view(ropeA.ap[hi, 0:cw]),
                     ropeB.view(ropeB.ap[hi, 0:cw]), ALU.add)
                S.dma("sp", TV(kfm_bufs[h], d_kfm[h][:, c0:c0 + cw]),
                      kst.view(kst.ap[:, 0:cw]))
                S.tt("dve", sqst.view(sqst.ap[:, 0:cw]), kst.view(kst.ap[:, 0:cw]),
                     kst.view(kst.ap[:, 0:cw]), ALU.mult)
                S.mm(bF.view(bF.ap[0:1, 0:cw]), ones_bf, sqst.view(sqst.ap[:, 0:cw]),
                     start=True, stop=True)
                S.red(k2tmp, bF.view(bF.ap[0:1, 0:cw]), "max")
                if ci == 0:
                    S.copy("dve", k2m.view(k2m.ap[0:1, h:h + 1]), k2tmp)
                else:
                    S.tt("dve", k2m.view(k2m.ap[0:1, h:h + 1]),
                         k2m.view(k2m.ap[0:1, h:h + 1]), k2tmp, ALU.max)
            for dj in range(2):
                dcol = dj * 512
                for jt in range(cw // 128):
                    pv = vps[jt % 2]
                    for kc in range(KC):
                        S.mm(pv, xc.view(xc3[:, kc, jt * 128:(jt + 1) * 128]),
                             wv_half.view(wv_half.ap[:, kc * 1024 + dcol:kc * 1024 + dcol + 512]),
                             start=(kc == 0), stop=(kc == KC - 1))
                    vst = vst_db[jt % 2]
                    S.copy("act", vst, pv)
                    S.dma("sp", TV(vtm_bufs[2 * pas + dj],
                                   d_vtm[c0 + jt * 128:c0 + (jt + 1) * 128,
                                         1024 * pas + dcol:1024 * pas + dcol + 512]), vst)

    xq_bf = tv("xq_bf", P_b[:, 0:16384])
    S.alias([xq_bf.buf], [wk_half.buf])
    pt_slots = [tv(f"pt{i}", P_b[:, 16384 + i * 512:16384 + (i + 1) * 512]) for i in range(4)]
    k2sq = tv("k2sq", P_b[:, 18432:21120])
    kh_db = [tv(f"kh{i}", P_b[:, 21120 + i * 2688:21120 + (i + 1) * 2688]) for i in range(2)]
    q2sq = tv("q2sq", P_b[:, 26496:27520])
    S.alias([b.buf for b in pt_slots] + [k2sq.buf, kh_db[0].buf, kh_db[1].buf, q2sq.buf],
            [wk_half.buf])
    ctx_all = tv("ctx_all", Q_b[:, 0:16384])
    vh_db = [tv(f"vh{i}", Q_b[:, 16384 + i * 2688:16384 + (i + 1) * 2688]) for i in range(2)]
    q_sb = tv("q_sb", Q_b[:, 21760:22784])
    wq_db = [tv(f"wqs{i}", Q_b[:, 22784 + i * 2048:22784 + (i + 1) * 2048]) for i in range(2)]
    S.alias([ctx_all.buf, vh_db[0].buf, vh_db[1].buf, q_sb.buf, wq_db[0].buf, wq_db[1].buf],
            [wv_half.buf])
    cosq = tv("cosq", S_f[:, 0:1024])
    sinq = tv("sinq", S_f[:, 1024:2048])
    rl_row = tv("rl_row", S_f[0:1, 2048:2560])
    rlB = tv("rlB", S_f[:, 2560:3072])
    ropeA2 = tv("ropeA2", S_f[:, 3072:3584])
    ropeB2 = tv("ropeB2", S_f[:, 3584:4096])

    xq4 = xq_bf.ap.rearrange("p (kc t) -> p kc t", kc=KC)
    S.dma("pool", xq_bf.view(xq4[:, 0:8]), d_xkv[:, 0:8, 1024:2048])
    S.dma("pool", xq_bf.view(xq4[:, 8:16]), d_xkv[:, 8:16, 1024:2048])
    S.dma("sp", cosq, d_ck[:, 1024:2048])
    S.dma("sp", sinq, d_sk[:, 1024:2048])

    vtm_r = d_vtm.rearrange("(j p) d -> p j d", p=128)
    lo, hi = slice(0, 64), slice(64, 128)
    for h in range(16):
        kh = kh_db[h % 2]
        vh = vh_db[h % 2]
        wqs = wq_db[h % 2]
        S.emit("sp", lambda e, o=kh.ap, i=d_kfm[h]: e.dma_start(out=o, in_=i),
               reads=[kfm_bufs[h]], writes=[kh.buf], dma=True)
        S.emit("act", lambda e, o=vh.ap.rearrange("p (j f) -> p j f", f=128),
               i=vtm_r[:, :, h * 128:(h + 1) * 128]: e.dma_start(out=o, in_=i),
               reads=[vtm_bufs[h // 4]], writes=[vh.buf], dma=True)
        S.dma("pool", wqs, d_wq[h].rearrange("p kc f -> p (kc f)"))
        for ch in range(2):
            pq = bE
            for kc in range(KC):
                S.mm(pq, wqs.view(wqs.ap[:, kc * 128:(kc + 1) * 128]),
                     xq_bf.view(xq4[:, kc, ch * 512:(ch + 1) * 512]),
                     start=(kc == 0), stop=(kc == KC - 1))
            qsl = q_sb.view(q_sb.ap[:, ch * 512:(ch + 1) * 512])
            ccol = ch * 512
            S.tt("dve", ropeA2.view(ropeA2.ap[lo, :]), pq.view(pq.ap[lo, :]),
                 cosq.view(cosq.ap[lo, ccol:ccol + 512]), ALU.mult)
            S.tt("dve", ropeB2.view(ropeB2.ap[lo, :]), pq.view(pq.ap[hi, :]),
                 sinq.view(sinq.ap[lo, ccol:ccol + 512]), ALU.mult)
            S.tt("dve", q_sb.view(q_sb.ap[lo, ch * 512:(ch + 1) * 512]),
                 ropeA2.view(ropeA2.ap[lo, :]), ropeB2.view(ropeB2.ap[lo, :]), ALU.subtract)
            S.tt("dve", ropeA2.view(ropeA2.ap[hi, :]), pq.view(pq.ap[hi, :]),
                 cosq.view(cosq.ap[hi, ccol:ccol + 512]), ALU.mult)
            S.tt("dve", ropeB2.view(ropeB2.ap[hi, :]), pq.view(pq.ap[lo, :]),
                 sinq.view(sinq.ap[hi, ccol:ccol + 512]), ALU.mult)
            S.tt("dve", q_sb.view(q_sb.ap[hi, ch * 512:(ch + 1) * 512]),
                 ropeA2.view(ropeA2.ap[hi, :]), ropeB2.view(ropeB2.ap[hi, :]), ALU.add)
            q2v = q2sq.view(q2sq.ap[:, 0:512])
            S.tt("dve", q2v, qsl, qsl, ALU.mult)
            S.mm(bD.view(bD.ap[0:1, 0:512]), ones_bf, q2v, start=True, stop=True)
            S.red(rows.view(rows.ap[0:1, 0:1]), bD.view(bD.ap[0:1, 0:512]), "max")
            S.tt("dve", rows.view(rows.ap[0:1, 1:2]), rows.view(rows.ap[0:1, 0:1]),
                 k2m.view(k2m.ap[0:1, h:h + 1]), ALU.add)
            S.ts("dve", rows.view(rows.ap[0:1, 2:3]), rows.view(rows.ap[0:1, 1:2]),
                 -0.5 * SCALE, ALU.mult)
            S.mm(bF.view(bF.ap[:, 0:1]), ones_row, rows.view(rows.ap[0:1, 2:3]),
                 start=True, stop=True)
            S.copy("dve", bias_sb[ch], bF.view(bF.ap[:, 0:1]))
        if "q0" in taps and h == 0:
            S.dma("sp", tap_aps["q0"], q_sb)
        for ch in range(2):
            qs_tv = q_sb.view(q_sb.ap[:, ch * 512:(ch + 1) * 512])
            slots = SLOTS[ch]
            n_s = len(slots)
            batches = [slots[i:i + 2] for i in range(0, n_s, 2)]
            gsi = 0
            for bi, bat in enumerate(batches):
                banks = [bA0, bA1] if bi % 2 == 0 else [bB0, bB1]
                ps_ap = psA.ap() if bi % 2 == 0 else psB.ap()
                for si2, s in enumerate(bat):
                    S.mm(banks[si2], kh.view(kh.ap[:, s * 128:(s + 1) * 128]), qs_tv,
                         start=True, stop=True)
                pslot0 = (bi % 2) * 2
                out_ap = P_b[:, 16384 + pslot0 * 512:16384 + (pslot0 + len(bat)) * 512]
                in_ap = ps_ap[:, 0:len(bat) * 512]
                wr = [pt_slots[pslot0 + k].buf for k in range(len(bat))]
                rd = [banks[k].buf for k in range(len(bat))] + [bias_sb[ch].buf]
                S.emit("act", lambda e, o=out_ap, i=in_ap, b=bias_sb[ch].ap:
                       e.activation(o, i, AF.Exp, bias=b, scale=SCALE),
                       reads=rd, writes=wr)
                for si2, s in enumerate(bat):
                    ptv_ = pt_slots[pslot0 + si2]
                    if mask_need[ch][gsi]:
                        mview = masks_sb.ap[:, (ch * NSLOT + gsi) * 8:(ch * NSLOT + gsi + 1) * 8]
                        mb = mview.unsqueeze(2).broadcast_to([128, 8, 64])
                        eng = "dve" if (gsi % 2 == 0) else "pool"
                        S.emit(eng, lambda e, o=ptv_.ap.rearrange("p (a b) -> p a b", a=8),
                               i=ptv_.ap.rearrange("p (a b) -> p a b", a=8), m=mb:
                               e.tensor_tensor(o, i, m, ALU.mult),
                               reads=[ptv_.buf, masks_sb.buf], writes=[ptv_.buf])
                    S.mm(bD.view(bD.ap[0:1, 0:512]), ones_bf, ptv_,
                         start=(gsi == 0), stop=(gsi == n_s - 1))
                    S.mm(bC, vh.view(vh.ap[:, s * 128:(s + 1) * 128]), ptv_,
                         start=(gsi == 0), stop=(gsi == n_s - 1))
                    gsi += 1
            S.recip(rl_row, bD.view(bD.ap[0:1, 0:512]))
            S.mm(bE, ones_row, rl_row, start=True, stop=True)
            S.copy("dve", rlB, bE)
            S.tt("dve", ctx_all.view(ctx_all.ap[:, h * 1024 + ch * 512:h * 1024 + (ch + 1) * 512]),
                 bC, rlB, ALU.mult)
    if "ctx" in taps:
        S.dma("sp", tap_aps["ctx"], ctx_all)

    xr = tv("xr", R_f[:, 0:8192])
    S.alias([xr.buf], [x_db[0].buf, x_db[1].buf, cs_db[0][0].buf, cs_db[0][1].buf,
                       cs_db[1][0].buf, cs_db[1][1].buf, ropeA.buf, ropeB.buf])
    wo_db = [tv(f"wo{i}", S_b[:, i * 2048:(i + 1) * 2048]) for i in range(2)]
    r2bf_db = [tv(f"r2bf{i}", S_b[:, 4096 + i * 512:4096 + (i + 1) * 512]) for i in range(2)]
    sqbf_db = [tv(f"sqbf{i}", S_b[:, 5120 + i * 512:5120 + (i + 1) * 512]) for i in range(2)]
    mrow = tv("mrow", S_f[0:1, 3072:3584])
    msqrow = tv("msqrow", S_f[0:1, 3584:4096])
    arow = tv("arow", S_f[0:1, 4096:4608])
    brow = tv("brow", S_f[0:1, 4608:5120])
    aB = tv("aB", S_f[:, 5120:5632])
    bB = tv("bB", S_f[:, 5632:6144])
    S.alias([wo_db[0].buf, wo_db[1].buf, r2bf_db[0].buf, r2bf_db[1].buf,
             sqbf_db[0].buf, sqbf_db[1].buf, mrow.buf, msqrow.buf, arow.buf, brow.buf,
             aB.buf, bB.buf],
            [cosq.buf, sinq.buf, rl_row.buf, rlB.buf, ropeA2.buf, ropeB2.buf])
    r1n_bf = tv("r1n_bf", P_b[:, 0:16384])
    S.alias([r1n_bf.buf], [xq_bf.buf] + [b.buf for b in pt_slots] +
            [k2sq.buf, kh_db[0].buf, kh_db[1].buf, q2sq.buf])

    xr3 = xr.ap.rearrange("p (fc t) -> p fc t", fc=16)
    orot = [bA0, bA1, bB0, bB1]
    for ch in range(2):
        S.dma("sp", xr.view(xr3), d_xkv[:, :, 1024 + ch * 512:1024 + (ch + 1) * 512])
        for fc in range(16):
            wos = wo_db[fc % 2]
            S.dma("pool", wos, d_wo[fc].rearrange("p kc f -> p (kc f)"))
            po = orot[fc % 4]
            for kc in range(KC):
                S.mm(po, wos.view(wos.ap[:, kc * 128:(kc + 1) * 128]),
                     ctx_all.view(ctx_all.ap[:, kc * 1024 + ch * 512:kc * 1024 + (ch + 1) * 512]),
                     start=(kc == 0), stop=(kc == KC - 1))
            xt = xr.view(xr3[:, fc])
            S.tt("dve", xt, xt, po, ALU.add)
        if "r1" in taps and ch == 0:
            S.dma("sp", tap_aps["r1"], xr)
        for fc in range(16):
            xt = xr.view(xr3[:, fc])
            rb = r2bf_db[fc % 2]
            sq = sqbf_db[fc % 2]
            S.copy("dve", rb, xt)
            S.tt("dve", sq, rb, rb, ALU.mult)
            S.mm(bD.view(bD.ap[0:1, 0:512]), ones_bf, rb, start=(fc == 0), stop=(fc == 15))
            S.mm(bC.view(bC.ap[0:1, 0:512]), ones_bf, sq, start=(fc == 0), stop=(fc == 15))
        S.ts("dve", mrow, bD.view(bD.ap[0:1, 0:512]), 1.0 / 2048.0, ALU.mult)
        S.ts("dve", msqrow, bC.view(bC.ap[0:1, 0:512]), 1.0 / 2048.0, ALU.mult)
        S.tt("dve", arow, mrow, mrow, ALU.mult)
        S.tt("dve", msqrow, msqrow, arow, ALU.subtract)
        S.ts("dve", msqrow, msqrow, EPS, ALU.add)
        S.act(msqrow, msqrow, AF.Sqrt)
        S.recip(arow, msqrow)
        S.tt("dve", brow, mrow, arow, ALU.mult)
        S.ts("dve", brow, brow, -1.0, ALU.mult)
        S.mm(bE, ones_row, arow, start=True, stop=True)
        S.copy("dve", aB, bE)
        S.mm(bF, ones_row, brow, start=True, stop=True)
        S.copy("dve", bB, bF)
        for fc in range(16):
            xt = xr.view(xr3[:, fc])
            S.tt("dve", xt, xt, aB, ALU.mult)
            S.tt("dve", xt, xt, bB, ALU.add)
            S.copy("dve", r1n_bf.view(r1n_bf.ap[:, fc * 1024 + ch * 512:fc * 1024 + (ch + 1) * 512]), xt)
        for tt_ in range(4):
            for fc in range(16):
                S.mm(bC.view(bC.ap[:, 0:8]), xr.view(xr3[:, fc, tt_ * 128:(tt_ + 1) * 128]),
                     rw_sb.view(rw_sb.ap.rearrange("p (k e) -> p k e", e=8)[:, fc]),
                     start=(fc == 0), stop=(fc == 15))
            S.copy("dve", lgs, bC.view(bC.ap[:, 0:8]))
            til = ch * 4 + tt_
            if "logits" in taps:
                S.copy("dve", tv("lgtap", C_f[:, 810 + til * 8:818 + til * 8]), lgs)
            S.red(m1c, lgs, "max")
            S.red(m1n, lgs, "max", negate=True)
            S.ts("dve", t2a, lgs, m1c, ALU.is_equal)
            S.stt("dve", t2b, t2a, -1e9, lgs, ALU.mult, ALU.add)
            S.red(m2v, t2b, "max")
            S.ts("dve", t2a, lgs, m2v, ALU.is_ge)
            S.act(ptl, lgs, AF.Exp, bias=m1n)
            S.tt("dve", t2b, ptl, t2a, ALU.mult)
            S.red(denr, t2b, "sum")
            S.recip(rdenr, denr)
            S.ts("dve", comb.view(comb.ap[:, til * 8:(til + 1) * 8]), t2b, rdenr, ALU.mult)
    if "r1n" in taps:
        S.dma("sp", tap_aps["r1n"], r1n_bf)
    if "combine" in taps:
        S.dma("sp", tap_aps["combine"], comb)
    if "logits" in taps:
        S.dma("sp", tap_aps["logits"], C_f[:, 810:874])
    return _build_ffn(nc, S, locals())


def _build_ffn(nc, S, L):
    g = L
    d_gu, d_dn, d_shg, d_shu, d_shd, d_out = g["d_gu"], g["d_dn"], g["d_shg"], g["d_shu"], g["d_shd"], g["d_out"]
    S_f, S_b, Q_b, R_f = g["S_f"], g["S_b"], g["Q_b"], g["R_f"]
    bA0, bA1, bB0, bB1, bC, bD, bE, bF = g["bA0"], g["bA1"], g["bB0"], g["bB1"], g["bC"], g["bD"], g["bE"], g["bF"]
    ones_bf, ones_row, ident, comb, sgw_sb = g["ones_bf"], g["ones_row"], g["ident"], g["comb"], g["sgw_sb"]
    r1n_bf = g["r1n_bf"]
    tap_aps, taps = g["tap_aps"], g["taps"]
    tv = g["tv"]
    rows = g["rows"]

    r1n4 = r1n_bf.ap.rearrange("p (fc t) -> p fc t", fc=16)

    gsl_db = [tv(f"gsl{i}", S_b[:, i * 2048:(i + 1) * 2048]) for i in range(2)]
    usl_db = [tv(f"usl{i}", S_b[:, 4096 + i * 2048:4096 + (i + 1) * 2048]) for i in range(2)]
    dn_db = [tv(f"dnsl{i}", S_b[:, 8192 + i * 1408:8192 + (i + 1) * 1408]) for i in range(2)]
    gsil = tv("gsil", S_b[:, 11008:11520])
    colB = [tv("colB0", S_f[:, 5760:6272]), tv("colB1", S_f[:, 6272:6784])]
    rowX = tv("rowX", S_f[0:1, 6784:7296])
    S.alias([b.buf for b in gsl_db + usl_db + dn_db] + [gsil.buf, colB[0].buf, colB[1].buf, rowX.buf],
            [g["wo_db"][0].buf, g["wo_db"][1].buf, g["r2bf_db"][0].buf, g["r2bf_db"][1].buf,
             g["sqbf_db"][0].buf, g["sqbf_db"][1].buf, g["aB"].buf, g["bB"].buf,
             g["mrow"].buf, g["msqrow"].buf, g["arow"].buf, g["brow"].buf])
    hsh = tv("hsh", Q_b[:, 0:SH_FC * 512])
    hexp = tv("hexp", Q_b[:, 0:11 * 1024])
    S.alias([hsh.buf, hexp.buf], [g["ctx_all"].buf, g["vh_db"][0].buf, g["vh_db"][1].buf,
                                  g["q_sb"].buf, g["wq_db"][0].buf, g["wq_db"][1].buf])
    acc = tv("acc", R_f[:, 0:16384])
    S.alias([acc.buf], [g["xr"].buf])
    acc4 = acc.ap.rearrange("p (fc t) -> p fc t", fc=16)

    grot = [bA0, bA1]
    urot = [bB0, bB1]

    for ch in range(2):
        for kc in range(KC):
            S.mm(bD.view(bD.ap[0:1, 0:512]), sgw_sb.view(sgw_sb.ap[:, kc:kc + 1]),
                 r1n_bf.view(r1n4[:, kc, ch * 512:(ch + 1) * 512]),
                 start=(kc == 0), stop=(kc == KC - 1))
        S.act(rowX, bD.view(bD.ap[0:1, 0:512]), AF.Sigmoid)
        S.mm(bE, ones_row, rowX, start=True, stop=True)
        S.copy("dve", colB[ch], bE)
        for fc in range(SH_FC):
            gsl = gsl_db[fc % 2]
            usl = usl_db[fc % 2]
            S.dma("pool", gsl, d_shg[fc].rearrange("p kc f -> p (kc f)"))
            S.dma("pool", usl, d_shu[fc].rearrange("p kc f -> p (kc f)"))
            pg = grot[fc % 2]
            pu = urot[fc % 2]
            for kc in range(KC):
                rtv = r1n_bf.view(r1n4[:, kc, ch * 512:(ch + 1) * 512])
                S.mm(pg, gsl.view(gsl.ap[:, kc * 128:(kc + 1) * 128]), rtv,
                     start=(kc == 0), stop=(kc == KC - 1))
                S.mm(pu, usl.view(usl.ap[:, kc * 128:(kc + 1) * 128]), rtv,
                     start=(kc == 0), stop=(kc == KC - 1))
            S.act(gsil, pg, AF.Silu)
            S.tt("dve", hsh.view(hsh.ap[:, fc * 512:(fc + 1) * 512]), gsil, pu, ALU.mult)
        if "hsh" in taps and ch == 0:
            S.dma("sp", tap_aps["hsh"], hsh)
        for fc2 in range(DN_FC):
            for gq in range(4):
                dsl = dn_db[gq % 2]
                S.dma("pool", dsl,
                      d_shd[fc2][:, gq * DN_KC:(gq + 1) * DN_KC].rearrange("p kc f -> p (kc f)"))
                for kc in range(DN_KC):
                    kk = gq * DN_KC + kc
                    S.mm(bC, dsl.view(dsl.ap[:, kc * 128:(kc + 1) * 128]),
                         hsh.view(hsh.ap[:, kk * 512:(kk + 1) * 512]),
                         start=(kk == 0), stop=(kk == SHD_KC - 1))
            S.tt("dve", acc.view(acc4[:, fc2, ch * 512:(ch + 1) * 512]), bC, colB[ch], ALU.mult)

    for e in range(MOE_E):
        for ch in range(2):
            for tt_ in range(4):
                til = ch * 4 + tt_
                S.mm(bD.view(bD.ap[0:1, tt_ * 128:(tt_ + 1) * 128]),
                     comb.view(comb.ap[:, til * 8 + e:til * 8 + e + 1]), ident,
                     start=True, stop=True)
            S.copy("dve", rowX, bD.view(bD.ap[0:1, 0:512]))
            S.mm(bE, ones_row, rowX, start=True, stop=True)
            S.copy("dve", colB[ch], bE)
        for gp in range(11):
            gsl = gsl_db[gp % 2]
            usl = usl_db[gp % 2]
            S.dma("pool", gsl, d_gu[e, gp].rearrange("p kc f -> p (kc f)"))
            S.dma("pool", usl, d_gu[e, gp + 11].rearrange("p kc f -> p (kc f)"))
            for ch in range(2):
                pg = grot[ch]
                pu = urot[ch]
                for kc in range(KC):
                    rtv = r1n_bf.view(r1n4[:, kc, ch * 512:(ch + 1) * 512])
                    S.mm(pg, gsl.view(gsl.ap[:, kc * 128:(kc + 1) * 128]), rtv,
                         start=(kc == 0), stop=(kc == KC - 1))
                    S.mm(pu, usl.view(usl.ap[:, kc * 128:(kc + 1) * 128]), rtv,
                         start=(kc == 0), stop=(kc == KC - 1))
                S.act(gsil, pg, AF.Silu)
                ht = hexp.view(hexp.ap[:, gp * 1024 + ch * 512:gp * 1024 + (ch + 1) * 512])
                S.tt("dve", ht, gsil, pu, ALU.mult)
                S.tt("dve", ht, ht, colB[ch], ALU.mult)
        for fc2 in range(DN_FC):
            dsl = dn_db[fc2 % 2]
            S.dma("pool", dsl, d_dn[e, fc2].rearrange("p kc f -> p (kc f)"))
            for ch in range(2):
                for kc in range(DN_KC):
                    S.mm(bC, dsl.view(dsl.ap[:, kc * 128:(kc + 1) * 128]),
                         hexp.view(hexp.ap[:, kc * 1024 + ch * 512:kc * 1024 + (ch + 1) * 512]),
                         start=(kc == 0), stop=(kc == DN_KC - 1))
                at = acc.view(acc4[:, fc2, ch * 512:(ch + 1) * 512])
                S.tt("dve", at, at, bC, ALU.add)
    if "acc" in taps:
        S.dma("sp", tap_aps["acc"], acc)

    out_db = [tv(f"outst{i}", S_f[:, i * 512:(i + 1) * 512]) for i in range(2)]
    r2b_db = [tv(f"r2b{i}", S_b[:, 2048 + i * 512:2048 + (i + 1) * 512]) for i in range(2)]
    sq2_db = [tv(f"sq2{i}", S_b[:, 3072 + i * 512:3072 + (i + 1) * 512]) for i in range(2)]
    mrow2 = tv("mrow2", S_f[0:1, 2048 + 0:2048]) if False else g["mrow"]
    msqrow2 = g["msqrow"]; arow2 = g["arow"]; brow2 = g["brow"]; aB2 = g["aB"]; bB2 = g["bB"]
    S.alias([b.buf for b in out_db + r2b_db + sq2_db],
            [b.buf for b in gsl_db + usl_db] + [dn_db[0].buf, dn_db[1].buf, gsil.buf])
    S.alias([mrow2.buf, msqrow2.buf, arow2.buf, brow2.buf, aB2.buf, bB2.buf],
            [dn_db[0].buf, dn_db[1].buf, gsil.buf])
    for ch in range(2):
        for fc in range(16):
            at = acc.view(acc4[:, fc, ch * 512:(ch + 1) * 512])
            S.tt("dve", at, at, r1n_bf.view(r1n4[:, fc, ch * 512:(ch + 1) * 512]), ALU.add)
        for fc in range(16):
            at = acc.view(acc4[:, fc, ch * 512:(ch + 1) * 512])
            rb = r2b_db[fc % 2]
            sq = sq2_db[fc % 2]
            S.copy("dve", rb, at)
            S.tt("dve", sq, rb, rb, ALU.mult)
            S.mm(bD.view(bD.ap[0:1, 0:512]), ones_bf, rb, start=(fc == 0), stop=(fc == 15))
            S.mm(bC.view(bC.ap[0:1, 0:512]), ones_bf, sq, start=(fc == 0), stop=(fc == 15))
        S.ts("dve", mrow2, bD.view(bD.ap[0:1, 0:512]), 1.0 / 2048.0, ALU.mult)
        S.ts("dve", msqrow2, bC.view(bC.ap[0:1, 0:512]), 1.0 / 2048.0, ALU.mult)
        S.tt("dve", arow2, mrow2, mrow2, ALU.mult)
        S.tt("dve", msqrow2, msqrow2, arow2, ALU.subtract)
        S.ts("dve", msqrow2, msqrow2, EPS, ALU.add)
        S.act(msqrow2, msqrow2, AF.Sqrt)
        S.recip(arow2, msqrow2)
        S.tt("dve", brow2, mrow2, arow2, ALU.mult)
        S.ts("dve", brow2, brow2, -1.0, ALU.mult)
        S.mm(bE, ones_row, arow2, start=True, stop=True)
        S.copy("dve", aB2, bE)
        S.mm(bF, ones_row, brow2, start=True, stop=True)
        S.copy("dve", bB2, bF)
        for fc in range(16):
            at = acc.view(acc4[:, fc, ch * 512:(ch + 1) * 512])
            ot = out_db[fc % 2]
            S.tt("dve", ot, at, aB2, ALU.mult)
            S.tt("dve", ot, ot, bB2, ALU.add)
            S.dma("sp", TV(Buf(f"out{ch}_{fc}"), d_out[fc][:, ch * 512:(ch + 1) * 512]), ot)

    S.replay()
    return nc


_BUILD_CACHE = {}


def kernel(**inputs):
    in_maps, mask_need = stage_host(inputs)
    key = "base"
    if key not in _BUILD_CACHE:
        _BUILD_CACHE[key] = build_program(mask_need)
    nc = _BUILD_CACHE[key]
    res = run_bass_kernel_spmd(nc, in_maps, core_ids=list(range(8)))
    out = np.zeros((B, T, Cd), dtype=np.float32)
    for c in range(8):
        b, r = c // 4, c % 4
        fm = np.asarray(res.results[c]["out"]).reshape(Cd, QTOK)
        out[b, QTOK * r:QTOK * (r + 1), :] = fm.T
    return out


# version 03 (final)
# speedup vs baseline: 1.1484x; 1.1484x over previous
import math
import numpy as np
import ml_dtypes
import concourse.bass as bass
import concourse.mybir as mybir
from concourse.bass_utils import run_bass_kernel_spmd

F32 = mybir.dt.float32
BF16 = mybir.dt.bfloat16
AF = mybir.ActivationFunctionType
ALU = mybir.AluOpType
AX = mybir.AxisListType

B, T, Cd = 2, 4096, 2048
H, Dh = 16, 128
BLK = 64
NQB = 16
QTOK = NQB * BLK
SCALE = 1.0 / math.sqrt(Dh)
EPS = 1e-5

N_WIN_T, N_GLOB_T, N_DIL_T = 16, 1, 4
NT_W = N_WIN_T + N_GLOB_T + N_DIL_T
NKT = NT_W * 128
DIL_BLOCKS = [4, 8, 12, 16, 20, 24, 28, 28]

SLOTS = [list(range(0, 12)) + [16] + [17, 18, 19, 20],
         list(range(4, 16)) + [16] + [17, 18, 19, 20]]
NSLOT = 17

A1_CHUNKS = [(0, 512), (512, 512), (1024, 512), (1536, 512), (2048, 512), (2560, 128)]

MOE_E, I_MOE, I_SH = 8, 1408, 5632
GU_FC, DN_FC, DN_KC = 22, 16, 11
SH_FC, SHD_KC = 44, 44
KC = 16


def ws_blocks(r):
    win = [16 * r - 16 + i for i in range(32)]
    return win + [0, 1] + list(DIL_BLOCKS)


def block_allowed(qb, kb):
    if kb > qb:
        return False
    return (qb - kb <= 16) or (kb <= 1) or (kb % 4 == 0)


def make_masks():
    all_masks = []
    for r in range(4):
        wb = ws_blocks(r)
        m = np.zeros((2, NSLOT, 128, 8), dtype=np.float32)
        for ch in range(2):
            provider = {}
            for si, s in enumerate(SLOTS[ch]):
                for half in range(2):
                    kb = wb[2 * s + half]
                    if kb >= 0 and kb not in provider:
                        provider[kb] = (si, half)
            for si, s in enumerate(SLOTS[ch]):
                for half in range(2):
                    kb = wb[2 * s + half]
                    if kb < 0 or provider.get(kb) != (si, half):
                        continue
                    for q8 in range(8):
                        qb = 16 * r + 8 * ch + q8
                        if block_allowed(qb, kb):
                            m[ch, si, half * 64:(half + 1) * 64, q8] = 1.0
        all_masks.append(m)
    need = [[False] * NSLOT for _ in range(2)]
    for ch in range(2):
        for si in range(NSLOT):
            if any((all_masks[r][ch, si] != 1.0).any() for r in range(4)):
                need[ch][si] = True
    return all_masks, need


def rope_tables(positions):
    pos = np.asarray(positions, dtype=np.float64)
    inv = 1.0 / (10000.0 ** (np.arange(0, Dh, 2, dtype=np.float64) / Dh))
    inv2 = np.concatenate([inv, inv])
    ang = np.outer(inv2, np.clip(pos, 0, None))
    cos = np.cos(ang).astype(np.float32)
    sin = np.sin(ang).astype(np.float32)
    z = pos < 0
    cos[:, z] = 0.0
    sin[:, z] = 0.0
    return cos, sin


def retile_lhsT(wT, n_kc, n_fc):
    K, F = wT.shape
    assert K == n_kc * 128 and F == n_fc * 128
    return np.ascontiguousarray(
        wT.reshape(n_kc, 128, n_fc, 128).transpose(2, 1, 0, 3))


def stage_host(inputs):
    x = np.asarray(inputs["L0_attn_x"], dtype=np.float32)
    wq = np.asarray(inputs["L0_attn_w_q"], dtype=np.float32)
    wk = np.asarray(inputs["L0_attn_w_k"], dtype=np.float32)
    wv = np.asarray(inputs["L0_attn_w_v"], dtype=np.float32)
    wo = np.asarray(inputs["L0_attn_w_o"], dtype=np.float32)
    rww = np.asarray(inputs["L0_ffn_router_weight"], dtype=np.float32)
    gu = np.asarray(inputs["L0_ffn_gate_up_proj"], dtype=np.float32)
    dwn = np.asarray(inputs["L0_ffn_down_proj"], dtype=np.float32)
    shg = np.asarray(inputs["L0_ffn_sh_gate_w"], dtype=np.float32)
    shu = np.asarray(inputs["L0_ffn_sh_up_w"], dtype=np.float32)
    shd = np.asarray(inputs["L0_ffn_sh_down_w"], dtype=np.float32)
    sgw = np.asarray(inputs["L0_ffn_shared_gate_w"], dtype=np.float32)

    shared = {
        "wq_t": retile_lhsT(wq.T, KC, 16),
        "wk_t": retile_lhsT(wk.T, KC, 16),
        "wv_t": np.ascontiguousarray(wv.T.reshape(KC, 128, 2048).transpose(1, 0, 2)),
        "wo_t": retile_lhsT(wo.T, KC, 16),
        "rw_t": np.ascontiguousarray(rww.T.reshape(KC, 128, 8).transpose(1, 0, 2)),
        "sgw_t": np.ascontiguousarray(sgw.T.reshape(KC, 128, 1).transpose(1, 0, 2)),
        "gu_t": np.stack([retile_lhsT(gu[e].T, KC, GU_FC) for e in range(MOE_E)]),
        "dn_t": np.stack([retile_lhsT(dwn[e].T, DN_KC, DN_FC) for e in range(MOE_E)]),
        "shg_t": retile_lhsT(shg.T, KC, SH_FC),
        "shu_t": retile_lhsT(shu.T, KC, SH_FC),
        "shd_t": retile_lhsT(shd.T, SHD_KC, DN_FC),
    }
    masks, mask_need = make_masks()

    in_maps = []
    for c in range(8):
        b, r = c // 4, c % 4
        xT = x[b].T
        wb = ws_blocks(r)
        xkv = np.zeros((Cd, NKT), dtype=np.float32)
        positions = np.full(NKT, -1, dtype=np.int64)
        for i, blk in enumerate(wb):
            if blk >= 0:
                xkv[:, BLK * i:BLK * (i + 1)] = xT[:, BLK * blk:BLK * (blk + 1)]
                positions[BLK * i:BLK * (i + 1)] = np.arange(BLK * blk, BLK * (blk + 1))
        cosv, sinv = rope_tables(positions)
        m = {
            "xkv": np.ascontiguousarray(xkv.reshape(KC, 128, NKT).transpose(1, 0, 2)),
            "ck": cosv, "sk": sinv,
            "msk": np.ascontiguousarray(masks[r].transpose(2, 0, 1, 3).reshape(128, 2 * NSLOT * 8)),
        }
        m.update(shared)
        in_maps.append(m)
    return in_maps, mask_need


class Buf:
    __slots__ = ("name", "wtok", "rtoks")

    def __init__(self, name):
        self.name = name
        self.wtok = None
        self.rtoks = []


class TV:
    __slots__ = ("buf", "ap")

    def __init__(self, buf, ap):
        self.buf = buf
        self.ap = ap

    def __getitem__(self, idx):
        return TV(self.buf, self.ap[idx])

    def view(self, ap):
        return TV(self.buf, ap)


ENG2BLOCK = {"pe": "tensor", "act": "scalar", "dve": "vector", "pool": "gpsimd", "sp": "sync"}
QUEUE_OF = {"sp": "qsp", "act": "qact", "pool": "qpool"}


class Sched:
    def __init__(self, nc):
        self.nc = nc
        self.ops = {e: [] for e in ENG2BLOCK}
        self.cnt = {}
        self.obs = {e: {} for e in ENG2BLOCK}
        self.meta = {}
        self.sems = {}
        for s in ["pe", "act", "dve", "pool", "sp", "qsp", "qact", "qpool"]:
            self.sems[s] = nc.alloc_semaphore(f"sem_{s}")

    def _needs(self, reads, writes):
        needs = {}

        def add(tok):
            if tok is not None:
                s, v = tok
                if needs.get(s, 0) < v:
                    needs[s] = v
        for b in reads:
            if b is not None:
                add(b.wtok)
        for b in writes:
            if b is not None:
                add(b.wtok)
                for t in b.rtoks:
                    add(t)
        return needs

    def emit(self, eng, fn, reads=(), writes=(), dma=False):
        needs = self._needs(reads, writes)
        obs = self.obs[eng]
        waits = []
        for sem in sorted(needs):
            val = needs[sem]
            if eng == "pe" and sem == "pe":
                continue
            if obs.get(sem, 0) >= val:
                continue
            waits.append((sem, val))
            obs[sem] = val
            mclock = self.meta.get((sem, val))
            if mclock:
                for s2, v2 in mclock.items():
                    if obs.get(s2, 0) < v2:
                        obs[s2] = v2
        if dma:
            q = QUEUE_OF[eng]
            self.cnt[q] = self.cnt.get(q, 0) + 16
            tok = (q, self.cnt[q])
        else:
            self.cnt[eng] = self.cnt.get(eng, 0) + 1
            tok = (eng, self.cnt[eng])
        self.meta[tok] = dict(obs)
        self.ops[eng].append((waits, fn, tok))
        for b in writes:
            if b is not None:
                b.wtok = tok
                b.rtoks = []
        for b in reads:
            if b is not None:
                b.rtoks.append(tok)
        return tok

    def dma(self, eng, out_tv, in_tv, extra_reads=()):
        out_b = out_tv.buf if isinstance(out_tv, TV) else None
        in_b = in_tv.buf if isinstance(in_tv, TV) else None
        out_ap = out_tv.ap if isinstance(out_tv, TV) else out_tv
        in_ap = in_tv.ap if isinstance(in_tv, TV) else in_tv
        return self.emit(eng, lambda e, o=out_ap, i=in_ap: e.dma_start(out=o, in_=i),
                         reads=[in_b] + list(extra_reads), writes=[out_b], dma=True)

    def mm(self, out_tv, lhsT_tv, rhs_tv, start, stop):
        return self.emit("pe", lambda e, o=out_tv.ap, l=lhsT_tv.ap, r=rhs_tv.ap,
                         st=start, sp=stop: e.matmul(o, l, r, start=st, stop=sp),
                         reads=[lhsT_tv.buf, rhs_tv.buf], writes=[out_tv.buf])

    def act(self, out_tv, in_tv, func, bias=0.0, scale=1.0, accum=None):
        reads = [in_tv.buf]
        bias_ap = bias
        if isinstance(bias, TV):
            reads.append(bias.buf)
            bias_ap = bias.ap
        writes = [out_tv.buf]
        acc_ap = None
        if accum is not None:
            writes.append(accum.buf)
            acc_ap = accum.ap
        return self.emit("act", lambda e, o=out_tv.ap, i=in_tv.ap, f=func, b=bias_ap,
                         s=scale, a=acc_ap: e.activation(o, i, f, bias=b, scale=s, accum_out=a)
                         if a is not None else e.activation(o, i, f, bias=b, scale=s),
                         reads=reads, writes=writes)

    def tt(self, eng, out_tv, a_tv, b_tv, op):
        return self.emit(eng, lambda e, o=out_tv.ap, x=a_tv.ap, y=b_tv.ap, p=op:
                         e.tensor_tensor(o, x, y, p),
                         reads=[a_tv.buf, b_tv.buf], writes=[out_tv.buf])

    def ts(self, eng, out_tv, in_tv, s1, op, s2=None, op2=None):
        reads = [in_tv.buf]
        s1ap = s1
        if isinstance(s1, TV):
            reads.append(s1.buf)
            s1ap = s1.ap
        s2ap = s2
        if isinstance(s2, TV):
            reads.append(s2.buf)
            s2ap = s2.ap
        if op2 is None:
            return self.emit(eng, lambda e, o=out_tv.ap, i=in_tv.ap, a=s1ap, p=op:
                             e.tensor_scalar(o, i, a, None, p),
                             reads=reads, writes=[out_tv.buf])
        return self.emit(eng, lambda e, o=out_tv.ap, i=in_tv.ap, a=s1ap, b=s2ap, p=op, p2=op2:
                         e.tensor_scalar(o, i, a, b, p, p2),
                         reads=reads, writes=[out_tv.buf])

    def stt(self, eng, out_tv, in0_tv, scalar, in1_tv, op0, op1):
        reads = [in0_tv.buf, in1_tv.buf]
        sap = scalar
        if isinstance(scalar, TV):
            reads.append(scalar.buf)
            sap = scalar.ap
        return self.emit(eng, lambda e, o=out_tv.ap, i0=in0_tv.ap, s=sap, i1=in1_tv.ap,
                         p0=op0, p1=op1: e.scalar_tensor_tensor(o, i0, s, i1, p0, p1),
                         reads=reads, writes=[out_tv.buf])

    def copy(self, eng, out_tv, in_tv):
        if eng == "act":
            return self.emit(eng, lambda e, o=out_tv.ap, i=in_tv.ap: e.copy(o, i),
                             reads=[in_tv.buf], writes=[out_tv.buf])
        return self.emit(eng, lambda e, o=out_tv.ap, i=in_tv.ap: e.tensor_copy(o, i),
                         reads=[in_tv.buf], writes=[out_tv.buf])

    def red(self, out_tv, in_tv, op, negate=False):
        if op == "max":
            return self.emit("dve", lambda e, o=out_tv.ap, i=in_tv.ap, n=negate:
                             e.reduce_max(o, i, axis=AX.X, negate=n),
                             reads=[in_tv.buf], writes=[out_tv.buf])
        return self.emit("dve", lambda e, o=out_tv.ap, i=in_tv.ap:
                         e.reduce_sum(o, i, axis=AX.X),
                         reads=[in_tv.buf], writes=[out_tv.buf])

    def recip(self, out_tv, in_tv):
        return self.emit("dve", lambda e, o=out_tv.ap, i=in_tv.ap: e.reciprocal(o, i),
                         reads=[in_tv.buf], writes=[out_tv.buf])

    def memset(self, eng, out_tv, val):
        return self.emit(eng, lambda e, o=out_tv.ap, v=val: e.memset(o, v),
                         reads=[], writes=[out_tv.buf])

    def alias(self, new_bufs, old_bufs):
        toks = []
        for b in old_bufs:
            if b is None:
                continue
            if b.wtok:
                toks.append(b.wtok)
            toks.extend(b.rtoks)
        for nb in new_bufs:
            nb.rtoks = list(toks)
            nb.wtok = None

    def replay(self):
        nc = self.nc
        sems = self.sems
        with nc.Block() as block:
            for eng, section in ENG2BLOCK.items():
                ops = self.ops[eng]
                final = dict(self.cnt) if eng == "sp" else None

                def body(e, ops=ops, eng=eng, final=final):
                    for waits, fn, tok in ops:
                        for (s, v) in waits:
                            e.wait_ge(sems[s], v)
                        inst = fn(e)
                        inst.then_inc(sems[tok[0]], 16 if tok[0].startswith("q") else 1)
                    if final is not None:
                        for s, v in sorted(final.items()):
                            e.wait_ge(sems[s], v)
                getattr(block, section)(body)


def build_program(mask_need, taps=()):
    nc = bass.Bass("TRN2", target_bir_lowering=False)

    d_xkv = nc.dram_tensor("xkv", [128, KC, NKT], F32, kind="ExternalInput").ap()
    d_ck = nc.dram_tensor("ck", [128, NKT], F32, kind="ExternalInput").ap()
    d_sk = nc.dram_tensor("sk", [128, NKT], F32, kind="ExternalInput").ap()
    d_msk = nc.dram_tensor("msk", [128, 2 * NSLOT * 8], F32, kind="ExternalInput").ap()
    d_wq = nc.dram_tensor("wq_t", [16, 128, KC, 128], F32, kind="ExternalInput").ap()
    d_wk = nc.dram_tensor("wk_t", [16, 128, KC, 128], F32, kind="ExternalInput").ap()
    d_wv = nc.dram_tensor("wv_t", [128, KC, 2048], F32, kind="ExternalInput").ap()
    d_wo = nc.dram_tensor("wo_t", [16, 128, KC, 128], F32, kind="ExternalInput").ap()
    d_rw = nc.dram_tensor("rw_t", [128, KC, 8], F32, kind="ExternalInput").ap()
    d_sgw = nc.dram_tensor("sgw_t", [128, KC, 1], F32, kind="ExternalInput").ap()
    d_gu = nc.dram_tensor("gu_t", [MOE_E, GU_FC, 128, KC, 128], F32, kind="ExternalInput").ap()
    d_dn = nc.dram_tensor("dn_t", [MOE_E, DN_FC, 128, DN_KC, 128], F32, kind="ExternalInput").ap()
    d_shg = nc.dram_tensor("shg_t", [SH_FC, 128, KC, 128], F32, kind="ExternalInput").ap()
    d_shu = nc.dram_tensor("shu_t", [SH_FC, 128, KC, 128], F32, kind="ExternalInput").ap()
    d_shd = nc.dram_tensor("shd_t", [DN_FC, 128, SHD_KC, 128], F32, kind="ExternalInput").ap()
    d_out = nc.dram_tensor("out", [16, 128, QTOK], F32, kind="ExternalOutput").ap()

    kfm_kind = "ExternalOutput" if "kfm" in taps else "Internal"
    vtm_kind = "ExternalOutput" if "vtm" in taps else "Internal"
    d_kfm = nc.dram_tensor("kfm_d", [16, 128, NKT], BF16, kind=kfm_kind).ap()
    d_vtm = nc.dram_tensor("vtm_d", [NKT, 2048], BF16, kind=vtm_kind).ap()
    tap_aps = {}
    for tname, shape, dt_ in [("ctx", [128, 16 * QTOK], BF16),
                              ("r1n", [128, 16 * QTOK], BF16),
                              ("logits", [128, 64], F32),
                              ("combine", [128, 64], F32),
                              ("acc", [128, 16 * QTOK], F32),
                              ("hsh", [128, SH_FC * 512], BF16),
                              ("q0", [128, QTOK], BF16),
                              ("r1", [128, 16 * 512], F32)]:
        if tname in taps:
            tap_aps[tname] = nc.dram_tensor("tap_" + tname, shape, dt_, kind="ExternalOutput").ap()

    d_ones_row = nc.inline_tensor(np.ones((1, 128), dtype=np.float32), name="ones_row").ap()
    d_ident = nc.inline_tensor(np.eye(128, dtype=np.float32), name="ident_f").ap()

    sb_C = nc.alloc_sbuf_tensor("sb_C", [128, 1024], F32)
    sb_S = nc.alloc_sbuf_tensor("sb_S", [128, 7680], F32)
    sb_P = nc.alloc_sbuf_tensor("sb_P", [128, 13824], F32)
    sb_Q = nc.alloc_sbuf_tensor("sb_Q", [128, 13824], F32)
    sb_R = nc.alloc_sbuf_tensor("sb_R", [128, 16384], F32)
    C_f, C_b = sb_C.ap(), sb_C.ap().bitcast(BF16)
    S_f, S_b = sb_S.ap(), sb_S.ap().bitcast(BF16)
    P_f, P_b = sb_P.ap(), sb_P.ap().bitcast(BF16)
    Q_f, Q_b = sb_Q.ap(), sb_Q.ap().bitcast(BF16)
    R_f, R_b = sb_R.ap(), sb_R.ap().bitcast(BF16)

    psA = nc.alloc_psum_tensor("psA", [128, 1024], F32)
    psB = nc.alloc_psum_tensor("psB", [128, 1024], F32)
    psC = nc.alloc_psum_tensor("psC", [128, 512], F32)
    psD = nc.alloc_psum_tensor("psD", [128, 512], F32)
    psE = nc.alloc_psum_tensor("psE", [128, 512], F32)
    psF = nc.alloc_psum_tensor("psF", [128, 512], F32)

    S = Sched(nc)

    def tv(name, ap):
        return TV(Buf(name), ap)

    bA0 = tv("psA0", psA.ap()[:, 0:512]); bA1 = tv("psA1", psA.ap()[:, 512:1024])
    bB0 = tv("psB0", psB.ap()[:, 0:512]); bB1 = tv("psB1", psB.ap()[:, 512:1024])
    bC = tv("psC", psC.ap()); bD = tv("psD", psD.ap())
    bE = tv("psE", psE.ap()); bF = tv("psF", psF.ap())

    ones_bf = tv("ones_bf", C_b[:, 0:1])
    sgw_sb = tv("sgw_sb", C_b[:, 8:24])
    ones_row = tv("ones_row", C_f[0:1, 16:144])
    ident = tv("ident", C_f[:, 144:272])
    masks_sb = tv("masks", C_b[:, 544:816])
    rows = tv("rows", C_f[0:1, 544:560])
    k2m = tv("k2m", C_f[0:1, 560:576])
    bias_sb = [tv("bias0", C_f[:, 576:577]), tv("bias1", C_f[:, 577:578])]
    m1c = tv("m1c", C_f[:, 578:579]); m1n = tv("m1n", C_f[:, 579:580])
    m2v = tv("m2v", C_f[:, 580:581]); denr = tv("denr", C_f[:, 581:582])
    rdenr = tv("rdenr", C_f[:, 582:583])
    comb = tv("comb", C_f[:, 586:650])
    lgs = tv("lgs", C_f[:, 650:658])
    t2a = tv("t2a", C_f[:, 658:666])
    t2b = tv("t2b", C_f[:, 666:674])
    ptl = tv("ptl", C_f[:, 674:682])
    rw_sb = tv("rw_sb", C_f[:, 682:810])

    S.memset("pool", ones_bf, 1.0)
    S.dma("sp", ones_row, d_ones_row)
    S.dma("sp", ident, d_ident)
    S.dma("pool", masks_sb.view(masks_sb.ap), d_msk)
    S.dma("sp", rw_sb.view(rw_sb.ap.rearrange("p (k e) -> p k e", e=8)), d_rw)
    S.dma("pool", sgw_sb.view(sgw_sb.ap.rearrange("p (k o) -> p k o", o=1)), d_sgw)

    kfm_bufs = [Buf(f"kfm{h}") for h in range(16)]
    vtm_bufs = [Buf(f"vtm{g}") for g in range(4)]

    wk_half = tv("wk_half", P_b[:, 0:8 * KC * 128])
    wv_half = tv("wv_half", Q_b[:, 0:KC * 1024])
    x_db = [tv(f"x{i}", R_b[:, i * KC * 512:(i + 1) * KC * 512]) for i in range(2)]
    cs_db = [(tv(f"cos{i}", R_f[:, 8192 + i * 1024:8192 + i * 1024 + 512]),
              tv(f"sin{i}", R_f[:, 8192 + i * 1024 + 512:8192 + i * 1024 + 1024]))
             for i in range(2)]
    ropeA = tv("ropeA", R_f[:, 10240:10752])
    ropeB = tv("ropeB", R_f[:, 10752:11264])
    kst_db = [tv(f"kst{i}", R_b[:, 22528 + i * 512:22528 + (i + 1) * 512]) for i in range(2)]
    vst_db = [tv(f"vst{i}", R_b[:, 23552 + i * 512:23552 + (i + 1) * 512]) for i in range(2)]
    sqst = tv("sqst", R_b[:, 24576:25088])
    k2tmp = tv("k2tmp", C_f[0:1, 583:584])

    for pas in range(2):
        wk4 = wk_half.ap.rearrange("p (fc kc f) -> p fc kc f", fc=8, kc=KC)
        wk_src = d_wk[8 * pas:8 * pas + 8].rearrange("fc p kc f -> p fc kc f")
        S.dma("pool", wk_half.view(wk4[:, 0:4]), wk_src[:, 0:4])
        S.dma("pool", wk_half.view(wk4[:, 4:8]), wk_src[:, 4:8])
        wv3 = wv_half.ap.rearrange("p (kc d) -> p kc d", kc=KC)
        wv_src = d_wv[:, :, 1024 * pas:1024 * pas + 1024]
        S.dma("pool", wv_half.view(wv3[:, 0:8]), wv_src[:, 0:8])
        S.dma("pool", wv_half.view(wv3[:, 8:16]), wv_src[:, 8:16])
        kps = [bA0, bA1]
        vps = [bB0, bB1]
        for ci, (c0, cw) in enumerate(A1_CHUNKS):
            xc = x_db[ci % 2]
            xc3 = xc.ap.rearrange("p (kc t) -> p kc t", kc=KC)
            S.dma("pool", xc.view(xc3[:, :, 0:cw]), d_xkv[:, :, c0:c0 + cw])
            cosc, sinc = cs_db[ci % 2]
            S.dma("sp", cosc.view(cosc.ap[:, 0:cw]), d_ck[:, c0:c0 + cw])
            S.dma("sp", sinc.view(sinc.ap[:, 0:cw]), d_sk[:, c0:c0 + cw])
            lo, hi = slice(0, 64), slice(64, 128)
            for fi in range(8):
                h = 8 * pas + fi
                pk = kps[fi % 2]
                pko = pk.view(pk.ap[:, 0:cw])
                for kc in range(KC):
                    S.mm(pko, wk_half.view(wk_half.ap[:, (fi * KC + kc) * 128:(fi * KC + kc + 1) * 128]),
                         xc.view(xc3[:, kc, 0:cw]), start=(kc == 0), stop=(kc == KC - 1))
                kst = kst_db[fi % 2]
                S.tt("dve", ropeA.view(ropeA.ap[lo, 0:cw]), pk.view(pk.ap[lo, 0:cw]),
                     cosc.view(cosc.ap[lo, 0:cw]), ALU.mult)
                S.tt("dve", ropeB.view(ropeB.ap[lo, 0:cw]), pk.view(pk.ap[hi, 0:cw]),
                     sinc.view(sinc.ap[lo, 0:cw]), ALU.mult)
                S.tt("dve", kst.view(kst.ap[lo, 0:cw]), ropeA.view(ropeA.ap[lo, 0:cw]),
                     ropeB.view(ropeB.ap[lo, 0:cw]), ALU.subtract)
                S.tt("dve", ropeA.view(ropeA.ap[hi, 0:cw]), pk.view(pk.ap[hi, 0:cw]),
                     cosc.view(cosc.ap[hi, 0:cw]), ALU.mult)
                S.tt("dve", ropeB.view(ropeB.ap[hi, 0:cw]), pk.view(pk.ap[lo, 0:cw]),
                     sinc.view(sinc.ap[hi, 0:cw]), ALU.mult)
                S.tt("dve", kst.view(kst.ap[hi, 0:cw]), ropeA.view(ropeA.ap[hi, 0:cw]),
                     ropeB.view(ropeB.ap[hi, 0:cw]), ALU.add)
                S.dma("sp", TV(kfm_bufs[h], d_kfm[h][:, c0:c0 + cw]),
                      kst.view(kst.ap[:, 0:cw]))
                S.tt("dve", sqst.view(sqst.ap[:, 0:cw]), kst.view(kst.ap[:, 0:cw]),
                     kst.view(kst.ap[:, 0:cw]), ALU.mult)
                S.mm(bF.view(bF.ap[0:1, 0:cw]), ones_bf, sqst.view(sqst.ap[:, 0:cw]),
                     start=True, stop=True)
                S.red(k2tmp, bF.view(bF.ap[0:1, 0:cw]), "max")
                if ci == 0:
                    S.copy("dve", k2m.view(k2m.ap[0:1, h:h + 1]), k2tmp)
                else:
                    S.tt("dve", k2m.view(k2m.ap[0:1, h:h + 1]),
                         k2m.view(k2m.ap[0:1, h:h + 1]), k2tmp, ALU.max)
            for dj in range(2):
                dcol = dj * 512
                for jt in range(cw // 128):
                    pv = vps[jt % 2]
                    for kc in range(KC):
                        S.mm(pv, xc.view(xc3[:, kc, jt * 128:(jt + 1) * 128]),
                             wv_half.view(wv_half.ap[:, kc * 1024 + dcol:kc * 1024 + dcol + 512]),
                             start=(kc == 0), stop=(kc == KC - 1))
                    vst = vst_db[jt % 2]
                    S.copy("act", vst, pv)
                    S.dma("sp", TV(vtm_bufs[2 * pas + dj],
                                   d_vtm[c0 + jt * 128:c0 + (jt + 1) * 128,
                                         1024 * pas + dcol:1024 * pas + dcol + 512]), vst)

    xq_bf = tv("xq_bf", P_b[:, 0:16384])
    S.alias([xq_bf.buf], [wk_half.buf])
    pt_slots = [tv(f"pt{i}", P_b[:, 16384 + i * 512:16384 + (i + 1) * 512]) for i in range(4)]
    k2sq = tv("k2sq", P_b[:, 18432:21120])
    kh_db = [tv(f"kh{i}", P_b[:, 21120 + i * 2688:21120 + (i + 1) * 2688]) for i in range(2)]
    q2sq = tv("q2sq", P_b[:, 26496:27520])
    S.alias([b.buf for b in pt_slots] + [k2sq.buf, kh_db[0].buf, kh_db[1].buf, q2sq.buf],
            [wk_half.buf])
    ctx_all = tv("ctx_all", Q_b[:, 0:16384])
    vh_db = [tv(f"vh{i}", Q_b[:, 16384 + i * 2688:16384 + (i + 1) * 2688]) for i in range(2)]
    q_sb = tv("q_sb", Q_b[:, 21760:22784])
    wq_db = [tv(f"wqs{i}", Q_b[:, 22784 + i * 2048:22784 + (i + 1) * 2048]) for i in range(2)]
    S.alias([ctx_all.buf, vh_db[0].buf, vh_db[1].buf, q_sb.buf, wq_db[0].buf, wq_db[1].buf],
            [wv_half.buf])
    cosq = tv("cosq", S_f[:, 0:1024])
    sinq = tv("sinq", S_f[:, 1024:2048])
    rl_row = tv("rl_row", S_f[0:1, 2048:2560])
    rlB = tv("rlB", S_f[:, 2560:3072])
    ropeA2 = tv("ropeA2", S_f[:, 3072:3584])
    ropeB2 = tv("ropeB2", S_f[:, 3584:4096])

    xq4 = xq_bf.ap.rearrange("p (kc t) -> p kc t", kc=KC)
    S.dma("pool", xq_bf.view(xq4[:, 0:8]), d_xkv[:, 0:8, 1024:2048])
    S.dma("pool", xq_bf.view(xq4[:, 8:16]), d_xkv[:, 8:16, 1024:2048])
    S.dma("sp", cosq, d_ck[:, 1024:2048])
    S.dma("sp", sinq, d_sk[:, 1024:2048])

    vtm_r = d_vtm.rearrange("(j p) d -> p j d", p=128)
    lo, hi = slice(0, 64), slice(64, 128)
    for h in range(16):
        kh = kh_db[h % 2]
        vh = vh_db[h % 2]
        wqs = wq_db[h % 2]
        S.emit("sp", lambda e, o=kh.ap, i=d_kfm[h]: e.dma_start(out=o, in_=i),
               reads=[kfm_bufs[h]], writes=[kh.buf], dma=True)
        S.emit("act", lambda e, o=vh.ap.rearrange("p (j f) -> p j f", f=128),
               i=vtm_r[:, :, h * 128:(h + 1) * 128]: e.dma_start(out=o, in_=i),
               reads=[vtm_bufs[h // 4]], writes=[vh.buf], dma=True)
        S.dma("pool", wqs.view(wqs.ap.rearrange("p (kc f) -> p kc f", kc=KC)), d_wq[h])
        for ch in range(2):
            pq = bE
            for kc in range(KC):
                S.mm(pq, wqs.view(wqs.ap[:, kc * 128:(kc + 1) * 128]),
                     xq_bf.view(xq4[:, kc, ch * 512:(ch + 1) * 512]),
                     start=(kc == 0), stop=(kc == KC - 1))
            qsl = q_sb.view(q_sb.ap[:, ch * 512:(ch + 1) * 512])
            ccol = ch * 512
            S.tt("dve", ropeA2.view(ropeA2.ap[lo, :]), pq.view(pq.ap[lo, :]),
                 cosq.view(cosq.ap[lo, ccol:ccol + 512]), ALU.mult)
            S.tt("dve", ropeB2.view(ropeB2.ap[lo, :]), pq.view(pq.ap[hi, :]),
                 sinq.view(sinq.ap[lo, ccol:ccol + 512]), ALU.mult)
            S.tt("dve", q_sb.view(q_sb.ap[lo, ch * 512:(ch + 1) * 512]),
                 ropeA2.view(ropeA2.ap[lo, :]), ropeB2.view(ropeB2.ap[lo, :]), ALU.subtract)
            S.tt("dve", ropeA2.view(ropeA2.ap[hi, :]), pq.view(pq.ap[hi, :]),
                 cosq.view(cosq.ap[hi, ccol:ccol + 512]), ALU.mult)
            S.tt("dve", ropeB2.view(ropeB2.ap[hi, :]), pq.view(pq.ap[lo, :]),
                 sinq.view(sinq.ap[hi, ccol:ccol + 512]), ALU.mult)
            S.tt("dve", q_sb.view(q_sb.ap[hi, ch * 512:(ch + 1) * 512]),
                 ropeA2.view(ropeA2.ap[hi, :]), ropeB2.view(ropeB2.ap[hi, :]), ALU.add)
            q2v = q2sq.view(q2sq.ap[:, 0:512])
            S.tt("dve", q2v, qsl, qsl, ALU.mult)
            S.mm(bD.view(bD.ap[0:1, 0:512]), ones_bf, q2v, start=True, stop=True)
            S.red(rows.view(rows.ap[0:1, 0:1]), bD.view(bD.ap[0:1, 0:512]), "max")
            S.tt("dve", rows.view(rows.ap[0:1, 1:2]), rows.view(rows.ap[0:1, 0:1]),
                 k2m.view(k2m.ap[0:1, h:h + 1]), ALU.add)
            S.ts("dve", rows.view(rows.ap[0:1, 2:3]), rows.view(rows.ap[0:1, 1:2]),
                 -0.5 * SCALE, ALU.mult)
            S.mm(bF.view(bF.ap[:, 0:1]), ones_row, rows.view(rows.ap[0:1, 2:3]),
                 start=True, stop=True)
            S.copy("dve", bias_sb[ch], bF.view(bF.ap[:, 0:1]))
        if "q0" in taps and h == 0:
            S.dma("sp", tap_aps["q0"], q_sb)
        for ch in range(2):
            qs_tv = q_sb.view(q_sb.ap[:, ch * 512:(ch + 1) * 512])
            slots = SLOTS[ch]
            n_s = len(slots)
            batches = [slots[i:i + 2] for i in range(0, n_s, 2)]

            def emit_lctx(bat, pslot0, g0):
                for si2, s in enumerate(bat):
                    gs = g0 + si2
                    ptv_ = pt_slots[pslot0 + si2]
                    if mask_need[ch][gs]:
                        mview = masks_sb.ap[:, (ch * NSLOT + gs) * 8:(ch * NSLOT + gs + 1) * 8]
                        mb = mview.unsqueeze(2).broadcast_to([128, 8, 64])
                        eng = "dve" if (gs % 2 == 0) else "pool"
                        S.emit(eng, lambda e, o=ptv_.ap.rearrange("p (a b) -> p a b", a=8),
                               i=ptv_.ap.rearrange("p (a b) -> p a b", a=8), m=mb:
                               e.tensor_tensor(o, i, m, ALU.mult),
                               reads=[ptv_.buf, masks_sb.buf], writes=[ptv_.buf])
                    S.mm(bD.view(bD.ap[0:1, 0:512]), ones_bf, ptv_,
                         start=(gs == 0), stop=(gs == n_s - 1))
                    S.mm(bC, vh.view(vh.ap[:, s * 128:(s + 1) * 128]), ptv_,
                         start=(gs == 0), stop=(gs == n_s - 1))

            pend = None
            gsi = 0
            for bi, bat in enumerate(batches):
                banks = [bA0, bA1] if bi % 2 == 0 else [bB0, bB1]
                ps_ap = psA.ap() if bi % 2 == 0 else psB.ap()
                for si2, s in enumerate(bat):
                    S.mm(banks[si2], kh.view(kh.ap[:, s * 128:(s + 1) * 128]), qs_tv,
                         start=True, stop=True)
                pslot0 = (bi % 2) * 2
                out_ap = P_b[:, 16384 + pslot0 * 512:16384 + (pslot0 + len(bat)) * 512]
                in_ap = ps_ap[:, 0:len(bat) * 512]
                wr = [pt_slots[pslot0 + k].buf for k in range(len(bat))]
                rd = [banks[k].buf for k in range(len(bat))] + [bias_sb[ch].buf]
                S.emit("act", lambda e, o=out_ap, i=in_ap, b=bias_sb[ch].ap:
                       e.activation(o, i, AF.Exp, bias=b, scale=SCALE),
                       reads=rd, writes=wr)
                if pend is not None:
                    emit_lctx(*pend)
                pend = (bat, pslot0, gsi)
                gsi += len(bat)
            emit_lctx(*pend)
            S.recip(rl_row, bD.view(bD.ap[0:1, 0:512]))
            S.mm(bE, ones_row, rl_row, start=True, stop=True)
            S.copy("dve", rlB, bE)
            S.tt("dve", ctx_all.view(ctx_all.ap[:, h * 1024 + ch * 512:h * 1024 + (ch + 1) * 512]),
                 bC, rlB, ALU.mult)
    if "ctx" in taps:
        S.dma("sp", tap_aps["ctx"], ctx_all)

    xr = tv("xr", R_f[:, 0:8192])
    S.alias([xr.buf], [x_db[0].buf, x_db[1].buf, cs_db[0][0].buf, cs_db[0][1].buf,
                       cs_db[1][0].buf, cs_db[1][1].buf, ropeA.buf, ropeB.buf])
    wo_db = [tv(f"wo{i}", S_b[:, i * 2048:(i + 1) * 2048]) for i in range(2)]
    r2bf_db = [tv(f"r2bf{i}", S_b[:, 4096 + i * 512:4096 + (i + 1) * 512]) for i in range(2)]
    sqbf_db = [tv(f"sqbf{i}", S_b[:, 5120 + i * 512:5120 + (i + 1) * 512]) for i in range(2)]
    mrow = tv("mrow", S_f[0:1, 3072:3584])
    msqrow = tv("msqrow", S_f[0:1, 3584:4096])
    arow = tv("arow", S_f[0:1, 4096:4608])
    brow = tv("brow", S_f[0:1, 4608:5120])
    aB = tv("aB", S_f[:, 5120:5632])
    bB = tv("bB", S_f[:, 5632:6144])
    S.alias([wo_db[0].buf, wo_db[1].buf, r2bf_db[0].buf, r2bf_db[1].buf,
             sqbf_db[0].buf, sqbf_db[1].buf, mrow.buf, msqrow.buf, arow.buf, brow.buf,
             aB.buf, bB.buf],
            [cosq.buf, sinq.buf, rl_row.buf, rlB.buf, ropeA2.buf, ropeB2.buf])
    r1n_bf = tv("r1n_bf", P_b[:, 0:16384])
    S.alias([r1n_bf.buf], [xq_bf.buf] + [b.buf for b in pt_slots] +
            [k2sq.buf, kh_db[0].buf, kh_db[1].buf, q2sq.buf])

    xr3 = xr.ap.rearrange("p (fc t) -> p fc t", fc=16)
    orot = [bA0, bA1, bB0, bB1]
    for ch in range(2):
        S.dma("sp", xr.view(xr3), d_xkv[:, :, 1024 + ch * 512:1024 + (ch + 1) * 512])
        for fc in range(16):
            wos = wo_db[fc % 2]
            S.dma("pool", wos.view(wos.ap.rearrange("p (kc f) -> p kc f", kc=KC)), d_wo[fc])
            po = orot[fc % 4]
            for kc in range(KC):
                S.mm(po, wos.view(wos.ap[:, kc * 128:(kc + 1) * 128]),
                     ctx_all.view(ctx_all.ap[:, kc * 1024 + ch * 512:kc * 1024 + (ch + 1) * 512]),
                     start=(kc == 0), stop=(kc == KC - 1))
            xt = xr.view(xr3[:, fc])
            S.tt("dve", xt, xt, po, ALU.add)
        if "r1" in taps and ch == 0:
            S.dma("sp", tap_aps["r1"], xr)
        for fc in range(16):
            xt = xr.view(xr3[:, fc])
            rb = r2bf_db[fc % 2]
            sq = sqbf_db[fc % 2]
            S.copy("dve", rb, xt)
            S.tt("dve", sq, rb, rb, ALU.mult)
            S.mm(bD.view(bD.ap[0:1, 0:512]), ones_bf, rb, start=(fc == 0), stop=(fc == 15))
            S.mm(bC.view(bC.ap[0:1, 0:512]), ones_bf, sq, start=(fc == 0), stop=(fc == 15))
        S.ts("dve", mrow, bD.view(bD.ap[0:1, 0:512]), 1.0 / 2048.0, ALU.mult)
        S.ts("dve", msqrow, bC.view(bC.ap[0:1, 0:512]), 1.0 / 2048.0, ALU.mult)
        S.tt("dve", arow, mrow, mrow, ALU.mult)
        S.tt("dve", msqrow, msqrow, arow, ALU.subtract)
        S.ts("dve", msqrow, msqrow, EPS, ALU.add)
        S.act(msqrow, msqrow, AF.Sqrt)
        S.recip(arow, msqrow)
        S.tt("dve", brow, mrow, arow, ALU.mult)
        S.ts("dve", brow, brow, -1.0, ALU.mult)
        S.mm(bE, ones_row, arow, start=True, stop=True)
        S.copy("dve", aB, bE)
        S.mm(bF, ones_row, brow, start=True, stop=True)
        S.copy("dve", bB, bF)
        for fc in range(16):
            xt = xr.view(xr3[:, fc])
            S.tt("dve", xt, xt, aB, ALU.mult)
            S.tt("dve", xt, xt, bB, ALU.add)
            S.copy("dve", r1n_bf.view(r1n_bf.ap[:, fc * 1024 + ch * 512:fc * 1024 + (ch + 1) * 512]), xt)
        for tt_ in range(4):
            for fc in range(16):
                S.mm(bC.view(bC.ap[:, 0:8]), xr.view(xr3[:, fc, tt_ * 128:(tt_ + 1) * 128]),
                     rw_sb.view(rw_sb.ap.rearrange("p (k e) -> p k e", e=8)[:, fc]),
                     start=(fc == 0), stop=(fc == 15))
            S.copy("dve", lgs, bC.view(bC.ap[:, 0:8]))
            til = ch * 4 + tt_
            if "logits" in taps:
                S.copy("dve", tv("lgtap", C_f[:, 810 + til * 8:818 + til * 8]), lgs)
            S.red(m1c, lgs, "max")
            S.red(m1n, lgs, "max", negate=True)
            S.ts("dve", t2a, lgs, m1c, ALU.is_equal)
            S.stt("dve", t2b, t2a, -1e9, lgs, ALU.mult, ALU.add)
            S.red(m2v, t2b, "max")
            S.ts("dve", t2a, lgs, m2v, ALU.is_ge)
            S.act(ptl, lgs, AF.Exp, bias=m1n)
            S.tt("dve", t2b, ptl, t2a, ALU.mult)
            S.red(denr, t2b, "sum")
            S.recip(rdenr, denr)
            S.ts("dve", comb.view(comb.ap[:, til * 8:(til + 1) * 8]), t2b, rdenr, ALU.mult)
    if "r1n" in taps:
        S.dma("sp", tap_aps["r1n"], r1n_bf)
    if "combine" in taps:
        S.dma("sp", tap_aps["combine"], comb)
    if "logits" in taps:
        S.dma("sp", tap_aps["logits"], C_f[:, 810:874])
    return _build_ffn(nc, S, locals())


def _build_ffn(nc, S, L):
    g = L
    d_gu, d_dn, d_shg, d_shu, d_shd, d_out = g["d_gu"], g["d_dn"], g["d_shg"], g["d_shu"], g["d_shd"], g["d_out"]
    S_f, S_b, Q_b, R_f = g["S_f"], g["S_b"], g["Q_b"], g["R_f"]
    bA0, bA1, bB0, bB1, bC, bD, bE, bF = g["bA0"], g["bA1"], g["bB0"], g["bB1"], g["bC"], g["bD"], g["bE"], g["bF"]
    ones_bf, ones_row, ident, comb, sgw_sb = g["ones_bf"], g["ones_row"], g["ident"], g["comb"], g["sgw_sb"]
    r1n_bf = g["r1n_bf"]
    tap_aps, taps = g["tap_aps"], g["taps"]
    tv = g["tv"]
    rows = g["rows"]

    r1n4 = r1n_bf.ap.rearrange("p (fc t) -> p fc t", fc=16)

    gsl_db = [tv(f"gsl{i}", S_b[:, i * 2048:(i + 1) * 2048]) for i in range(2)]
    usl_db = [tv(f"usl{i}", S_b[:, 4096 + i * 2048:4096 + (i + 1) * 2048]) for i in range(2)]
    dn_db = [tv(f"dnsl{i}", S_b[:, 8192 + i * 1408:8192 + (i + 1) * 1408]) for i in range(2)]
    gsil = tv("gsil", S_b[:, 11008:11520])
    colB = [tv("colB0", S_f[:, 5760:6272]), tv("colB1", S_f[:, 6272:6784])]
    rowX = tv("rowX", S_f[0:1, 6784:7296])
    S.alias([b.buf for b in gsl_db + usl_db + dn_db] + [gsil.buf, colB[0].buf, colB[1].buf, rowX.buf],
            [g["wo_db"][0].buf, g["wo_db"][1].buf, g["r2bf_db"][0].buf, g["r2bf_db"][1].buf,
             g["sqbf_db"][0].buf, g["sqbf_db"][1].buf, g["aB"].buf, g["bB"].buf,
             g["mrow"].buf, g["msqrow"].buf, g["arow"].buf, g["brow"].buf])
    hsh = tv("hsh", Q_b[:, 0:SH_FC * 512])
    hexp = tv("hexp", Q_b[:, 0:11 * 1024])
    S.alias([hsh.buf, hexp.buf], [g["ctx_all"].buf, g["vh_db"][0].buf, g["vh_db"][1].buf,
                                  g["q_sb"].buf, g["wq_db"][0].buf, g["wq_db"][1].buf])
    acc = tv("acc", R_f[:, 0:16384])
    S.alias([acc.buf], [g["xr"].buf])
    acc4 = acc.ap.rearrange("p (fc t) -> p fc t", fc=16)

    grot = [bA0, bA1]
    urot = [bB0, bB1]

    for ch in range(2):
        for kc in range(KC):
            S.mm(bD.view(bD.ap[0:1, 0:512]), sgw_sb.view(sgw_sb.ap[:, kc:kc + 1]),
                 r1n_bf.view(r1n4[:, kc, ch * 512:(ch + 1) * 512]),
                 start=(kc == 0), stop=(kc == KC - 1))
        S.act(rowX, bD.view(bD.ap[0:1, 0:512]), AF.Sigmoid)
        S.mm(bE, ones_row, rowX, start=True, stop=True)
        S.copy("dve", colB[ch], bE)
        for fc in range(SH_FC):
            gsl = gsl_db[fc % 2]
            usl = usl_db[fc % 2]
            S.dma("pool", gsl.view(gsl.ap.rearrange("p (kc f) -> p kc f", kc=KC)), d_shg[fc])
            S.dma("pool", usl.view(usl.ap.rearrange("p (kc f) -> p kc f", kc=KC)), d_shu[fc])
            pg = grot[fc % 2]
            pu = urot[fc % 2]
            for kc in range(KC):
                rtv = r1n_bf.view(r1n4[:, kc, ch * 512:(ch + 1) * 512])
                S.mm(pg, gsl.view(gsl.ap[:, kc * 128:(kc + 1) * 128]), rtv,
                     start=(kc == 0), stop=(kc == KC - 1))
                S.mm(pu, usl.view(usl.ap[:, kc * 128:(kc + 1) * 128]), rtv,
                     start=(kc == 0), stop=(kc == KC - 1))
            S.act(gsil, pg, AF.Silu)
            S.tt("dve", hsh.view(hsh.ap[:, fc * 512:(fc + 1) * 512]), gsil, pu, ALU.mult)
        if "hsh" in taps and ch == 0:
            S.dma("sp", tap_aps["hsh"], hsh)
        for fc2 in range(DN_FC):
            for gq in range(4):
                dsl = dn_db[gq % 2]
                S.dma("pool", dsl.view(dsl.ap.rearrange("p (kc f) -> p kc f", kc=DN_KC)),
                      d_shd[fc2][:, gq * DN_KC:(gq + 1) * DN_KC])
                pdn = bC if fc2 % 2 == 0 else bD
                for kc in range(DN_KC):
                    kk = gq * DN_KC + kc
                    S.mm(pdn, dsl.view(dsl.ap[:, kc * 128:(kc + 1) * 128]),
                         hsh.view(hsh.ap[:, kk * 512:(kk + 1) * 512]),
                         start=(kk == 0), stop=(kk == SHD_KC - 1))
            pdn = bC if fc2 % 2 == 0 else bD
            S.tt("dve", acc.view(acc4[:, fc2, ch * 512:(ch + 1) * 512]), pdn, colB[ch], ALU.mult)

    for e in range(MOE_E):
        for ch in range(2):
            for tt_ in range(4):
                til = ch * 4 + tt_
                S.mm(bD.view(bD.ap[0:1, tt_ * 128:(tt_ + 1) * 128]),
                     comb.view(comb.ap[:, til * 8 + e:til * 8 + e + 1]), ident,
                     start=True, stop=True)
            S.copy("dve", rowX, bD.view(bD.ap[0:1, 0:512]))
            S.mm(bE, ones_row, rowX, start=True, stop=True)
            S.copy("dve", colB[ch], bE)
        for gp in range(11):
            gsl = gsl_db[gp % 2]
            usl = usl_db[gp % 2]
            S.dma("pool", gsl.view(gsl.ap.rearrange("p (kc f) -> p kc f", kc=KC)), d_gu[e, gp])
            S.dma("pool", usl.view(usl.ap.rearrange("p (kc f) -> p kc f", kc=KC)), d_gu[e, gp + 11])
            for ch in range(2):
                pg = grot[ch]
                pu = urot[ch]
                for kc in range(KC):
                    rtv = r1n_bf.view(r1n4[:, kc, ch * 512:(ch + 1) * 512])
                    S.mm(pg, gsl.view(gsl.ap[:, kc * 128:(kc + 1) * 128]), rtv,
                         start=(kc == 0), stop=(kc == KC - 1))
                    S.mm(pu, usl.view(usl.ap[:, kc * 128:(kc + 1) * 128]), rtv,
                         start=(kc == 0), stop=(kc == KC - 1))
                S.act(gsil, pg, AF.Silu)
                ht = hexp.view(hexp.ap[:, gp * 1024 + ch * 512:gp * 1024 + (ch + 1) * 512])
                S.tt("dve", ht, gsil, pu, ALU.mult)
                S.tt("dve", ht, ht, colB[ch], ALU.mult)
        for fc2 in range(DN_FC):
            dsl = dn_db[fc2 % 2]
            S.dma("pool", dsl.view(dsl.ap.rearrange("p (kc f) -> p kc f", kc=DN_KC)), d_dn[e, fc2])
            for ch in range(2):
                pdn = bC if ch == 0 else bD
                for kc in range(DN_KC):
                    S.mm(pdn, dsl.view(dsl.ap[:, kc * 128:(kc + 1) * 128]),
                         hexp.view(hexp.ap[:, kc * 1024 + ch * 512:kc * 1024 + (ch + 1) * 512]),
                         start=(kc == 0), stop=(kc == DN_KC - 1))
                at = acc.view(acc4[:, fc2, ch * 512:(ch + 1) * 512])
                S.tt("dve", at, at, pdn, ALU.add)
    if "acc" in taps:
        S.dma("sp", tap_aps["acc"], acc)

    out_db = [tv(f"outst{i}", S_f[:, i * 512:(i + 1) * 512]) for i in range(2)]
    r2b_db = [tv(f"r2b{i}", S_b[:, 2048 + i * 512:2048 + (i + 1) * 512]) for i in range(2)]
    sq2_db = [tv(f"sq2{i}", S_b[:, 3072 + i * 512:3072 + (i + 1) * 512]) for i in range(2)]
    mrow2 = tv("mrow2", S_f[0:1, 2048 + 0:2048]) if False else g["mrow"]
    msqrow2 = g["msqrow"]; arow2 = g["arow"]; brow2 = g["brow"]; aB2 = g["aB"]; bB2 = g["bB"]
    S.alias([b.buf for b in out_db + r2b_db + sq2_db],
            [b.buf for b in gsl_db + usl_db] + [dn_db[0].buf, dn_db[1].buf, gsil.buf])
    S.alias([mrow2.buf, msqrow2.buf, arow2.buf, brow2.buf, aB2.buf, bB2.buf],
            [dn_db[0].buf, dn_db[1].buf, gsil.buf])
    for ch in range(2):
        for fc in range(16):
            at = acc.view(acc4[:, fc, ch * 512:(ch + 1) * 512])
            S.tt("dve", at, at, r1n_bf.view(r1n4[:, fc, ch * 512:(ch + 1) * 512]), ALU.add)
        for fc in range(16):
            at = acc.view(acc4[:, fc, ch * 512:(ch + 1) * 512])
            rb = r2b_db[fc % 2]
            sq = sq2_db[fc % 2]
            S.copy("dve", rb, at)
            S.tt("dve", sq, rb, rb, ALU.mult)
            S.mm(bD.view(bD.ap[0:1, 0:512]), ones_bf, rb, start=(fc == 0), stop=(fc == 15))
            S.mm(bC.view(bC.ap[0:1, 0:512]), ones_bf, sq, start=(fc == 0), stop=(fc == 15))
        S.ts("dve", mrow2, bD.view(bD.ap[0:1, 0:512]), 1.0 / 2048.0, ALU.mult)
        S.ts("dve", msqrow2, bC.view(bC.ap[0:1, 0:512]), 1.0 / 2048.0, ALU.mult)
        S.tt("dve", arow2, mrow2, mrow2, ALU.mult)
        S.tt("dve", msqrow2, msqrow2, arow2, ALU.subtract)
        S.ts("dve", msqrow2, msqrow2, EPS, ALU.add)
        S.act(msqrow2, msqrow2, AF.Sqrt)
        S.recip(arow2, msqrow2)
        S.tt("dve", brow2, mrow2, arow2, ALU.mult)
        S.ts("dve", brow2, brow2, -1.0, ALU.mult)
        S.mm(bE, ones_row, arow2, start=True, stop=True)
        S.copy("dve", aB2, bE)
        S.mm(bF, ones_row, brow2, start=True, stop=True)
        S.copy("dve", bB2, bF)
        for fc in range(16):
            at = acc.view(acc4[:, fc, ch * 512:(ch + 1) * 512])
            ot = out_db[fc % 2]
            S.tt("dve", ot, at, aB2, ALU.mult)
            S.tt("dve", ot, ot, bB2, ALU.add)
            S.dma("sp", TV(Buf(f"out{ch}_{fc}"), d_out[fc][:, ch * 512:(ch + 1) * 512]), ot)

    S.replay()
    return nc


_BUILD_CACHE = {}


def kernel(**inputs):
    in_maps, mask_need = stage_host(inputs)
    key = "base"
    if key not in _BUILD_CACHE:
        _BUILD_CACHE[key] = build_program(mask_need)
    nc = _BUILD_CACHE[key]
    res = run_bass_kernel_spmd(nc, in_maps, core_ids=list(range(8)))
    out = np.zeros((B, T, Cd), dtype=np.float32)
    for c in range(8):
        b, r = c // 4, c % 4
        fm = np.asarray(res.results[c]["out"]).reshape(Cd, QTOK)
        out[b, QTOK * r:QTOK * (r + 1), :] = fm.T
    return out
